# version 17 (final)
# baseline (speedup 1.0000x reference)
import numpy as np
from contextlib import ExitStack

import concourse.bass as bass
import concourse.mybir as mybir
from concourse.bass_utils import run_bass_kernel_spmd
from concourse.bass_isa import ReduceOp
from concourse import library_config

F32 = mybir.dt.float32
BF16 = mybir.dt.bfloat16
AF = mybir.ActivationFunctionType
ALU = mybir.AluOpType

H, HD, DM, S = 16, 64, 1024, 4096
NS, SC = 16, 128
TOK = NS * SC
NKC = S // 128
NW = 4
HID = 4096
NEG = -1.0e30


def Pfn(k):
    return (NKC - k + 1) // 2


KEND = [NKC - 1 - 8 * w for w in range(NW)]

ENGS = ("sync", "tensor", "vector", "scalar", "gpsimd")


class Sem:
    def __init__(self, nc, name):
        self._cm = nc.semaphore(name)
        self.h = self._cm.__enter__()
        self.n = 0

    def inc(self, k=1):
        self.n += k
        return self.n


class Builder:
    def __init__(self, nc):
        self.nc = nc
        self.ops = {e: [] for e in ENGS}

    def emit(self, eng, fn):
        self.ops[eng].append(fn)

    def wait(self, eng, sem, val):
        if val <= 0:
            return
        self.emit(eng, lambda e, s=sem.h, v=val: e.wait_ge(s, v))

    def run(self, eng, fn, inc=None, inc_by=1):
        if inc is None:
            self.emit(eng, fn)
            return None
        t = inc.inc(inc_by)
        self.emit(eng, lambda e, f=fn, s=inc.h, k=inc_by: f(e).then_inc(s, k))
        return t

    def srun(self, eng, fn, inc):
        self.wait(eng, inc, inc.n)
        return self.run(eng, fn, inc=inc)


def build_nc(stop=None, dbg=False):
    nc = bass.Bass("TRN2", target_bir_lowering=False, debug=False,
                   detect_race_conditions=False)
    dti = lambda name, shape: nc.dram_tensor(name, shape, F32, kind="ExternalInput").ap()
    x_seq = dti("x_seq", [S, DM])
    x_own = dti("x_own", [TOK, DM])
    x_ownT = dti("x_ownT", [8, 128, TOK])
    wqkvT = dti("wqkvT", [DM, 3 * DM])
    bqk = dti("bqk", [128, 16])
    bv = dti("bv", [128, DM])
    woT = dti("woT", [DM, DM])
    bo = dti("bo", [128, 8])
    ghost = dti("ghost", [1, H])
    gateT = dti("gateT", [8, 128, 2])
    w1s = dti("w1s", [DM, HID])
    b1s = dti("b1s", [128, 32])
    w2s = dti("w2s", [HID, DM])
    sb2 = dti("sb2", [128, 8])
    w1e0 = dti("w1e0", [DM, HID])
    w1e1 = dti("w1e1", [DM, HID])
    b1e = dti("b1e", [2, 128, 32])
    w2e0 = dti("w2e0", [HID, DM])
    w2e1 = dti("w2e1", [HID, DM])
    b2e = dti("b2e", [2, 128, 8])
    dtab = dti("dtab", [128, 2])
    out = nc.dram_tensor("out", [8, 128, TOK], F32, kind="ExternalOutput").ap()
    dts = lambda name, shape: nc.dram_tensor(name, shape, F32, kind="ExternalOutput" if dbg else "Internal").ap()
    q_hbm = nc.dram_tensor("q_hbm", [2, 8, 128, TOK], BF16,
                           kind="ExternalOutput" if dbg else "Internal").ap()
    xt_hbm = nc.dram_tensor("xt_hbm", [2, 8, 128, TOK], BF16, kind="ExternalOutput" if dbg else "Internal").ap()
    v_hbm = dts("v_hbm", [S, DM])
    a_hbm = dts("a_hbm", [8, 128, TOK])
    g_hbm = dts("g_hbm", [1, TOK])

    b = Builder(nc)
    with ExitStack() as ctx:
        sbuf = lambda name, shape, dt=F32: ctx.enter_context(nc.sbuf_tensor(name, shape, dt))
        psm = lambda name, shape: ctx.enter_context(nc.psum_tensor(name, shape, F32))
        sem = lambda name: Sem(nc, name)

        ld = sem("ld")
        st = sem("st")
        cp = sem("cp")
        lx = sem("lx")
        pe = sem("pe")
        sc = sem("sc")
        ve = sem("ve")
        gp = sem("gp")

        psA = [psm(f"psA{i}", [128, 512]) for i in range(4)]
        psB = [psm(f"psB{i}", [128, 1024]) for i in range(2)]

        iotB = sbuf("iotB", [128, 128])
        ident = sbuf("ident", [128, 128])
        masks = sbuf("masks", [128, 512], BF16)
        c_dt = sbuf("c_dt", [128, 2])
        c_bqk = sbuf("c_bqk", [128, 16])
        c_bo = sbuf("c_bo", [128, 8])
        c_gh = sbuf("c_gh", [1, H])
        c_b1s = sbuf("c_b1s", [128, 32])
        c_b1e = sbuf("c_b1e", [128, 64])
        c_rb1 = sbuf("c_rb1", [128, 64])
        c_cA = sbuf("c_cA", [128, 8])
        c_b2t = sbuf("c_b2t", [128, 16])
        lnst = sbuf("lnst", [128, 12])
        gtsb = sbuf("gtsb", [128, 16])
        gdw = sbuf("gdw", [128, 8])
        gdwb = sbuf("gdwb", [128, 16], BF16)
        ones1 = sbuf("ones1", [1, 128])
        unitv = sbuf("unitv", [128, 1])
        mrun = sbuf("mrun", [128, 104])

        xnTb = sbuf("xnTb", [128, 32768])
        big1 = sbuf("big1", [128, 8192])
        big2 = sbuf("big2", [128, 8192])
        big3 = sbuf("big3", [128, 2048])

        A = lambda t, o, n: t.ap()[:, o:o + n]
        Ab = lambda t, o, n: t.ap()[:, o:o + n].bitcast(BF16)
        xnTh = Ab(xnTb, 0, 16384)
        xnTl = Ab(xnTb, 16384, 16384)
        wbig = A(big1, 0, 2048)
        wbh = Ab(big1, 2048, 1024)
        wbl = Ab(big1, 3072, 1024)
        c_bvb = A(big1, 6144, 1024)
        qth = Ab(big1, 4096, 256)
        qtl = Ab(big1, 4352, 256)
        sqscr = A(big1, 7168, 1024)
        wvstg = A(big2, 0, 4096)
        wvh = Ab(big2, 4096, 2048)
        wvl = Ab(big2, 6144, 2048)
        xint = A(big2, 0, 2048).rearrange("p (b n) -> p b n", b=2)
        xn128 = A(big2, 2048, 2048).rearrange("p (b n) -> p b n", b=2)
        xnth = Ab(big2, 4096, 2048)
        xntl = Ab(big2, 6144, 2048)
        qtile = A(big3, 0, 512)
        vtile = A(big3, 512, 1024)
        kbs = [Ab(big1, 0, 2048), Ab(big1, 2048, 2048)]
        ubuf = [A(big1, 4096, 2048), A(big1, 6144, 2048)]
        qhi2 = Ab(big2, 0, 1024)
        qlo2 = Ab(big2, 1024, 1024)
        vh = A(big2, 2048, 2080)
        wkP = A(big2, 4128, 1024)
        f32stg = A(big2, 4128, 512)
        tmpb = Ab(big2, 4640, 256)
        lo0t = Ab(big2, 4896, 256)
        wkh = Ab(big2, 5152, 512)
        wkl = Ab(big2, 5664, 512)
        denb = big2.ap()[0:1, 6176:6688]
        asb65 = big2.ap()[0:65, 6688:7200]
        urow = big2.ap()[0:1, 7200:7712]
        emrow = big3.ap()[0:1, 0:2048]
        ACC = A(xnTb, 0, 16384)
        w1c = [(Ab(xnTb, 16384, 2048), Ab(xnTb, 18432, 2048)),
               (Ab(xnTb, 20480, 2048), Ab(xnTb, 22528, 2048))]
        w2c = [(Ab(xnTb, 24576, 2048), Ab(xnTb, 26624, 2048)),
               (Ab(xnTb, 28672, 2048), Ab(xnTb, 30720, 2048))]
        woA = A(xnTb, 16384, 8192)
        xot = [(Ab(big2, 0, 2048), Ab(big2, 2048, 2048)),
               (Ab(big2, 4096, 2048), Ab(big2, 6144, 2048))]
        atile = [A(big2, 0, 4096), A(big2, 4096, 4096)]
        hbh = Ab(big3, 0, 1024)
        hbl = Ab(big3, 1024, 1024)
        sgt = A(big1, 0, 512)
        obuf = A(big1, 512, 512)
        xt_in = A(big1, 1024, 512)
        gdif = big1.ap()[0:1, 1536:3584]
        gdtmp = big3.ap()[0:1, 1536:2048]
        I32 = mybir.dt.int32
        m1b = A(big1, 3584, 2048).bitcast(I32)
        Hbuf = [A(big1, 1536, 2048), A(big1, 5632, 2048)]
        relscr = A(big1, 7680, 512)
        hself = A(big1, 0, 512)
        stile = A(big3, 0, 512)
        gtile = A(big3, 512, 512)

        E = b.emit
        r3 = lambda ap, c=8: ap.rearrange("p (c n) -> p c n", c=c)

        def _(e):
            e.dma_start(out=c_bqk[:], in_=bqk[:]).then_inc(ld.h, 16)
            e.dma_start(out=c_bo[:], in_=bo[:]).then_inc(ld.h, 16)
            e.dma_start(out=c_gh[:], in_=ghost[:]).then_inc(ld.h, 16)
            e.dma_start(out=c_b1s[:], in_=b1s[:]).then_inc(ld.h, 16)
            e.dma_start(out=c_b1e[:, 0:32], in_=b1e[0]).then_inc(ld.h, 16)
            e.dma_start(out=c_b1e[:, 32:64], in_=b1e[1]).then_inc(ld.h, 16)
            e.dma_start(out=gtsb.ap().rearrange("p (c e) -> p c e", e=2), in_=gateT.rearrange("c p e -> p c e")).then_inc(ld.h, 16)
            e.dma_start(out=c_dt[:], in_=dtab[:]).then_inc(ld.h, 16)
            e.dma_start(out=c_bvb, in_=bv[:]).then_inc(ld.h, 16)
        E("sync", _)
        ld.n += 16 * 9
        ld_const = ld.n

        b.run("gpsimd", lambda e: e.iota(iotB[:], pattern=[[1, 128]], base=0,
                                         channel_multiplier=-1,
                                         allow_small_or_imprecise_dtypes=True), inc=gp)
        t_iot = gp.n
        b.run("gpsimd", lambda e: e.memset(ones1[:], 1.0), inc=gp)
        b.run("gpsimd", lambda e: e.memset(unitv[:], 0.0), inc=gp)
        b.wait("gpsimd", gp, gp.n)
        b.run("gpsimd", lambda e: e.memset(unitv[64:65, :], 1.0), inc=gp)
        t_gpinit = gp.n

        b.wait("vector", gp, t_iot)
        b.run("vector", lambda e: e.tensor_scalar(ident[:], iotB[:], 0.0, 1.0,
                                                  op0=ALU.is_equal, op1=ALU.mult), inc=ve)
        t_ident = ve.n
        b.wait("vector", ld, ld_const)
        b.run("vector", lambda e: e.tensor_tensor(
            gdw[:], gtsb.ap().rearrange("p (c e) -> p c e", e=2)[:, :, 1],
            gtsb.ap().rearrange("p (c e) -> p c e", e=2)[:, :, 0],
            op=ALU.subtract), inc=ve)
        t_gdw = ve.n
        b.wait("vector", ld, ld_const)
        for par in range(2):
            b.run("vector", lambda e, par=par: e.tensor_scalar(
                masks[:, 128 * par:128 * (par + 1)], iotB[:],
                c_dt[:, par:par + 1], NEG, op0=ALU.is_lt, op1=ALU.mult), inc=ve)
        for par in range(2):
            b.srun("vector", lambda e, par=par: e.tensor_scalar(
                lnst[:, 8:9], c_dt[:, par:par + 1], -1.0, 0.0,
                op0=ALU.mult, op1=ALU.add), inc=ve)
            b.srun("vector", lambda e, par=par: e.tensor_scalar(
                masks[:, 128 * (2 + par):128 * (3 + par)], iotB[:],
                lnst[:, 8:9], NEG, op0=ALU.is_gt, op1=ALU.mult), inc=ve)
        t_masks = ve.n

        def emit_ln(xin_ap, dbuf):
            s_ = lnst[:, 0:1]; mcol = lnst[:, 1:2]; vvcol = lnst[:, 2:3]
            ycol = lnst[:, 3:4]; t1 = lnst[:, 4:5]
            b.srun("vector", lambda e: e.reduce_sum(s_, xin_ap, axis=mybir.AxisListType.X), inc=ve)
            b.run("scalar", lambda e: e.activation(sqscr, xin_ap, AF.Square,
                                                   accum_out=vvcol), inc=sc)
            t_sq = sc.n
            b.wait("vector", sc, t_sq)
            b.srun("vector", lambda e: e.tensor_scalar(mcol, s_, 1.0 / DM, 0.0,
                                                       op0=ALU.mult, op1=ALU.add), inc=ve)
            b.run("vector", lambda e: e.tensor_scalar(vvcol, vvcol, 1.0 / DM, 0.0,
                                                      op0=ALU.mult, op1=ALU.add), inc=ve)
            b.srun("vector", lambda e: e.tensor_tensor(t1, mcol, mcol, op=ALU.mult), inc=ve)
            b.srun("vector", lambda e: e.tensor_tensor(vvcol, vvcol, t1, op=ALU.subtract), inc=ve)
            b.srun("vector", lambda e: e.tensor_scalar(vvcol, vvcol, 1.0e-5, 0.0,
                                                       op0=ALU.add, op1=ALU.add), inc=ve)
            tv = ve.n
            b.wait("scalar", ve, tv)
            b.wait("scalar", sc, sc.n)
            b.run("scalar", lambda e: e.sqrt(ycol, vvcol), inc=sc)
            b.wait("vector", sc, sc.n)
            b.srun("vector", lambda e: e.reciprocal(ycol, ycol), inc=ve)
            b.srun("vector", lambda e: e.tensor_tensor(t1, ycol, ycol, op=ALU.mult), inc=ve)
            b.srun("vector", lambda e: e.tensor_tensor(t1, t1, vvcol, op=ALU.mult), inc=ve)
            b.srun("vector", lambda e: e.tensor_scalar(t1, t1, -0.5, 1.5,
                                                       op0=ALU.mult, op1=ALU.add), inc=ve)
            b.srun("vector", lambda e: e.tensor_tensor(ycol, ycol, t1, op=ALU.mult), inc=ve)
            b.wait("vector", pe, xn_free[dbuf])
            b.srun("vector", lambda e: e.tensor_scalar(xn128[:, dbuf, :], xin_ap, mcol,
                                                       ycol, op0=ALU.subtract, op1=ALU.mult), inc=ve)
            return ve.n

        tpctr = [0]
        tpfree = [(0, 0), (0, 0)]
        xn_free = [0, 0]

        def emit_transpose(dbuf, dst_sel):
            for g in range(2):
                slot = tpctr[0] % 2
                tpctr[0] += 1
                pk = psA[2 + slot]
                b.wait("tensor", sc, tpfree[slot][0])
                b.wait("tensor", ve, tpfree[slot][1])
                for c4 in range(4):
                    c = 4 * g + c4
                    b.run("tensor", lambda e, c=c, c4=c4, pk=pk, dbuf=dbuf: e.transpose(
                        pk[:, 128 * c4:128 * (c4 + 1)],
                        xn128[:, dbuf, 128 * c:128 * (c + 1)], ident[:]),
                        inc=(pe if c4 == 3 else None))
                tmm = pe.n
                hi_ap, lo_ap = dst_sel(g)
                b.wait("scalar", pe, tmm)
                b.run("scalar", lambda e, pk=pk, hi_ap=hi_ap: e.copy(
                    hi_ap, pk.ap().rearrange("p (c n) -> p c n", c=4)), inc=sc)
                thi = sc.n
                b.wait("vector", sc, thi)
                b.run("vector", lambda e, pk=pk, hi_ap=hi_ap, lo_ap=lo_ap: e.scalar_tensor_tensor(
                    lo_ap, pk.ap().rearrange("p (c n) -> p c n", c=4), 1.0, hi_ap,
                    op0=ALU.mult, op1=ALU.subtract), inc=ve)
                tpfree[slot] = (sc.n, ve.n)
            xn_free[dbuf] = pe.n
            return sc.n

        r3h = lambda ap, c=8: ap.rearrange("p (c n) -> p c n", c=c)
        qstore = [0]
        xtstore = [0]
        qring2 = [0, 0]
        qtl_done = [0]
        ln_done = {}
        gd_ev = [0]
        gd_st = [0]
        wq_free = [0]
        sub = 0
        b.wait("vector", ve, t_gdw)
        b.srun("vector", lambda e: e.tensor_copy(gdwb[:, 0:8], gdw[:]), inc=ve)
        b.srun("vector", lambda e: e.scalar_tensor_tensor(
            gdwb[:, 8:16], gdw[:], 1.0, gdwb[:, 0:8],
            op0=ALU.mult, op1=ALU.subtract), inc=ve)
        t_gdwb = ve.n
        for t in range(4):
            for s4 in range(4):
                xb_ = sub % 2
                if sub >= 2:
                    b.wait("sync", ve, ln_done[sub - 2][0])
                    b.wait("sync", sc, ln_done[sub - 2][1])
                def _(e, t=t, s4=s4, xb_=xb_):
                    e.dma_start(out=xint[:, xb_, :],
                                in_=x_own[512 * t + 128 * s4:512 * t + 128 * (s4 + 1), :]
                                ).then_inc(ld.h, 16)
                E("sync", _)
                ld.n += 16
                b.wait("vector", ld, ld.n)
                b.wait("scalar", ld, ld.n)
                tln = emit_ln(xint[:, xb_, :], xb_)
                b.wait("tensor", ve, tln)
                b.wait("tensor", ve, t_ident)
                b.wait("scalar", st, xtstore[0])
                def dst_sel(g, s4=s4):
                    return (r3h(xnth)[:, 4 * g:4 * (g + 1), 128 * s4:128 * (s4 + 1)],
                            r3h(xntl)[:, 4 * g:4 * (g + 1), 128 * s4:128 * (s4 + 1)])
                tsc = emit_transpose(xb_, dst_sel)
                ln_done[sub] = (ve.n, sc.n)
                sub += 1
            t_xnTt = sc.n
            t_xnTtv = ve.n
            b.wait("sync", sc, t_xnTt)
            b.wait("sync", ve, t_xnTtv)
            def _(e, t=t):
                e.dma_start(out=xt_hbm.rearrange("l c p n -> p l c n")[:, 0, :, 512 * t:512 * (t + 1)],
                            in_=r3h(xnth)).then_inc(st.h, 16)
                e.dma_start(out=xt_hbm.rearrange("l c p n -> p l c n")[:, 1, :, 512 * t:512 * (t + 1)],
                            in_=r3h(xntl)).then_inc(st.h, 16)
            E("sync", _)
            st.n += 32
            xtstore[0] = st.n
            b.wait("tensor", sc, t_xnTt)
            b.wait("tensor", ve, max(gd_ev[0], t_gdwb, t_xnTtv))
            nmm = 0
            for c in range(8):
                for ph in range(3):
                    stat = gdwb[:, c:c + 1] if ph < 2 else gdwb[:, 8 + c:9 + c]
                    mov = (r3h(xnth)[:, c, :] if ph != 1 else r3h(xntl)[:, c, :])
                    nmm += 1
                    b.run("tensor", lambda e, stat=stat, mov=mov, nmm=nmm: e.matmul(
                        psA[0][0:1, 0:512], stat, mov,
                        start=(nmm == 1), stop=(nmm == 24)), inc=(pe if nmm == 24 else None))
            t_gd = pe.n
            b.wait("vector", pe, t_gd)
            b.wait("vector", st, gd_st[0])
            b.run("vector", lambda e: e.tensor_copy(gdtmp, psA[0][0:1, 0:512]), inc=ve)
            gd_ev[0] = ve.n
            b.wait("sync", ve, gd_ev[0])
            def _(e, t=t):
                e.dma_start(out=g_hbm[0:1, 512 * t:512 * (t + 1)], in_=gdtmp).then_inc(st.h, 16)
            E("sync", _)
            st.n += 16
            gd_st[0] = st.n
            for quarter in range(4):
                b.wait("sync", pe, wq_free[0])
                def _(e, quarter=quarter):
                    e.dma_start(out=r3h(wbig)[:, :, 0:256],
                                in_=wqkvT.rearrange("(c p) n -> p c n", p=128)[:, :, 256 * quarter:256 * (quarter + 1)]
                                ).then_inc(ld.h, 16)
                E("sync", _)
                ld.n += 16
                ld_wq = ld.n
                b.wait("vector", ld, ld_wq)
                b.wait("vector", pe, wq_free[0])
                b.srun("vector", lambda e: e.tensor_copy(r3h(wbh)[:, :, :], r3h(wbig)[:, :, 0:256]), inc=ve)
                b.srun("vector", lambda e: e.scalar_tensor_tensor(
                    r3h(wbl)[:, :, :], r3h(wbig)[:, :, 0:256], 1.0, r3h(wbh)[:, :, :],
                    op0=ALU.mult, op1=ALU.subtract), inc=ve)
                t_wq = ve.n
                b.wait("tensor", ve, t_wq)
                for q2 in range(2):
                    qc = 2 * quarter + q2
                    qs = qc % 2
                    b.wait("tensor", sc, qring2[qs])
                    nmm = 0
                    for c in range(8):
                        for ph in range(3):
                            stat = (r3h(wbh)[:, c, 128 * q2:128 * (q2 + 1)] if ph < 2
                                    else r3h(wbl)[:, c, 128 * q2:128 * (q2 + 1)])
                            mov = (r3h(xnth)[:, c, :] if ph != 1 else r3h(xntl)[:, c, :])
                            nmm += 1
                            b.run("tensor", lambda e, stat=stat, mov=mov, nmm=nmm, qs=qs: e.matmul(
                                psB[qs][:, 0:512], stat, mov,
                                start=(nmm == 1), stop=(nmm == 24)), inc=(pe if nmm == 24 else None))
                    tq = pe.n
                    b.wait("scalar", pe, tq)
                    b.wait("scalar", st, qstore[0])
                    b.wait("scalar", ve, qtl_done[0])
                    b.run("scalar", lambda e, qc=qc, qs=qs: e.activation(
                        qtile, psB[qs][:, 0:512], AF.Identity,
                        bias=c_bqk[:, qc:qc + 1], scale=1.0), inc=sc)
                    qring2[qs] = sc.n
                    b.srun("scalar", lambda e: e.activation(
                        qth, qtile, AF.Identity, bias=0.0, scale=1.0), inc=sc)
                    t_qth = sc.n
                    b.wait("vector", sc, t_qth)
                    b.run("vector", lambda e: e.scalar_tensor_tensor(
                        qtl, qtile, 1.0, qth, op0=ALU.mult, op1=ALU.subtract), inc=ve)
                    qtl_done[0] = ve.n
                    b.wait("sync", sc, t_qth)
                    b.wait("sync", ve, qtl_done[0])
                    def _(e, qc=qc, t=t):
                        e.dma_start(out=q_hbm[0, qc, :, 512 * t:512 * (t + 1)], in_=qth
                                    ).then_inc(st.h, 16)
                        e.dma_start(out=q_hbm[1, qc, :, 512 * t:512 * (t + 1)], in_=qtl
                                    ).then_inc(st.h, 16)
                    E("sync", _)
                    st.n += 32
                    qstore[0] = st.n
                wq_free[0] = pe.n
        t_gd_all = gd_ev[0]

        def finish():
            b.wait("sync", st, st.n)
            with nc.Block() as block:
                @block.sync
                def _(e):
                    for f in b.ops["sync"]:
                        f(e)

                @block.tensor
                def _(e):
                    for f in b.ops["tensor"]:
                        f(e)

                @block.vector
                def _(e):
                    for f in b.ops["vector"]:
                        f(e)

                @block.scalar
                def _(e):
                    for f in b.ops["scalar"]:
                        f(e)

                @block.gpsimd
                def _(e):
                    for f in b.ops["gpsimd"]:
                        f(e)
            return nc

        if stop == "A0":
            return finish()

        wvstg1 = A(big1, 0, 4096)
        b.wait("sync", pe, wq_free[0])
        def _(e):
            e.dma_start(out=r3h(wvstg1)[:, :, 0:512],
                        in_=wqkvT.rearrange("(c p) n -> p c n", p=128)[:, :, 2 * DM:2 * DM + 512]
                        ).then_inc(ld.h, 16)
        E("sync", _)
        ld.n += 16
        ld_wv0 = ld.n
        b.wait("vector", ld, ld_wv0)
        b.wait("vector", pe, wq_free[0])
        b.wait("vector", st, xtstore[0])
        b.wait("vector", sc, sc.n)
        b.srun("vector", lambda e: e.tensor_copy(r3h(wvh)[:, :, :], r3h(wvstg1)[:, :, 0:512]), inc=ve)
        b.srun("vector", lambda e: e.scalar_tensor_tensor(
            r3h(wvl)[:, :, :], r3h(wvstg1)[:, :, 0:512], 1.0, r3h(wvh)[:, :, :],
            op0=ALU.mult, op1=ALU.subtract), inc=ve)
        t_wv0 = ve.n
        vring = [0, 0]
        vst = [0]
        t_qevac_all = sc.n

        def emit_vmms(tc, half):
            hs = tc % 2
            b.wait("tensor", ve, max(vring[hs], t_wv0 if half == 0 else 0))
            if half == 0:
                b.wait("tensor", sc, t_qevac_all)
                b.wait("tensor", sc, ln_done[16 + tc][1])
                b.wait("tensor", ve, ln_done[16 + tc][0])
            nmm = 0
            for c in range(8):
                for ph in range(3):
                    stat = (r3h(xnTh)[:, c, 128 * tc:128 * (tc + 1)] if ph != 1
                            else r3h(xnTl)[:, c, 128 * tc:128 * (tc + 1)])
                    mov = (r3h(wvh)[:, c, :] if ph < 2 else r3h(wvl)[:, c, :])
                    nmm += 1
                    b.run("tensor", lambda e, stat=stat, mov=mov, nmm=nmm, hs=hs: e.matmul(
                        psB[hs][:, 0:512], stat, mov,
                        start=(nmm == 1), stop=(nmm == 24)), inc=(pe if nmm == 24 else None))
            return pe.n

        def emit_vevac(tc, half, tv_):
            hs = tc % 2
            b.wait("vector", pe, tv_)
            b.wait("vector", st, vst[0])
            b.run("vector", lambda e, half=half, hs=hs: e.scalar_tensor_tensor(
                vtile[:, 512 * half:512 * (half + 1)], psB[hs][:, 0:512], 1.0,
                c_bvb[:, 512 * half:512 * (half + 1)],
                op0=ALU.mult, op1=ALU.add), inc=ve)
            vring[hs] = ve.n
            b.wait("sync", ve, ve.n)
            def _(e, tc=tc, half=half):
                e.dma_start(out=v_hbm[128 * tc:128 * (tc + 1), 512 * half:512 * (half + 1)],
                            in_=vtile[:, 512 * half:512 * (half + 1)]).then_inc(st.h, 16)
            E("sync", _)
            st.n += 16
            vst[0] = st.n

        vpend = []
        for t in range(8):
            for s4 in range(4):
                xb_ = sub % 2
                b.wait("sync", ve, ln_done[sub - 2][0])
                b.wait("sync", sc, ln_done[sub - 2][1])
                def _(e, t=t, s4=s4, xb_=xb_):
                    e.dma_start(out=xint[:, xb_, :],
                                in_=x_seq[512 * t + 128 * s4:512 * t + 128 * (s4 + 1), :]
                                ).then_inc(ld.h, 16)
                E("sync", _)
                ld.n += 16
                b.wait("vector", ld, ld.n)
                b.wait("scalar", ld, ld.n)
                tln = emit_ln(xint[:, xb_, :], xb_)
                b.wait("tensor", ve, tln)
                col = 512 * t + 128 * s4
                def dst_sel(g, col=col):
                    return (r3h(xnTh)[:, 4 * g:4 * (g + 1), col:col + 128],
                            r3h(xnTl)[:, 4 * g:4 * (g + 1), col:col + 128])
                emit_transpose(xb_, dst_sel)
                ln_done[sub] = (ve.n, sc.n)
                sub += 1
                if vpend:
                    emit_vevac(*vpend.pop(0))
                tc = sub - 1 - 16
                vpend.append((tc, 0, emit_vmms(tc, 0)))
        while vpend:
            emit_vevac(*vpend.pop(0))
        t_xnT_done = sc.n
        t_xnT_donev = ve.n

        b.wait("sync", ve, t_wv0)
        def _(e):
            e.dma_start(out=r3h(wvstg1)[:, :, 0:512],
                        in_=wqkvT.rearrange("(c p) n -> p c n", p=128)[:, :, 2 * DM + 512:2 * DM + 1024]
                        ).then_inc(ld.h, 16)
        E("sync", _)
        ld.n += 16
        ld_wv = ld.n
        b.wait("vector", ld, ld_wv)
        b.wait("vector", pe, pe.n)
        b.srun("vector", lambda e: e.tensor_copy(r3h(wvh)[:, :, :], r3h(wvstg1)[:, :, 0:512]), inc=ve)
        b.srun("vector", lambda e: e.scalar_tensor_tensor(
            r3h(wvl)[:, :, :], r3h(wvstg1)[:, :, 0:512], 1.0, r3h(wvh)[:, :, :],
            op0=ALU.mult, op1=ALU.subtract), inc=ve)
        t_wv1 = ve.n
        b.wait("tensor", ve, t_wv1)
        for tc in range(32):
            tv_ = emit_vmms(tc, 1)
            emit_vevac(tc, 1, tv_)
        t_v_done = st.n

        b.wait("gpsimd", sc, t_xnT_done)
        b.wait("gpsimd", ve, ve.n)
        b.run("gpsimd", lambda e: e.memset(
            vh.rearrange("p (k c) -> p k c", c=65)[:, :, 64:65], 1.0), inc=gp)
        t_vones = gp.n
        b.wait("scalar", ve, ve.n)

        prA_state = [0, 0, 0, 0]
        sringS = [0, 0]
        sringQ = [0, 0]
        emrow_rd = [0, 0, 0, 0]
        emc_rd = [0] * NS
        uguard_pe = [0, 0]
        t_aev = [0, 0, 0, 0]
        asb_free = [0]

        for hp in range(8):
            b.wait("sync", pe, pe.n)
            def _(e, hp=hp):
                e.dma_start(out=r3(wkP)[:, :, :],
                            in_=wqkvT.rearrange("(c p) n -> p c n", p=128)[:, :, DM + 128 * hp:DM + 128 * (hp + 1)]
                            ).then_inc(ld.h, 16)
            E("sync", _)
            ld.n += 16
            ld_wk = ld.n
            b.wait("vector", ld, ld_wk)
            b.wait("vector", pe, pe.n)
            b.srun("vector", lambda e: e.tensor_copy(r3h(wkh)[:, :, :], r3(wkP)[:, :, :]), inc=ve)
            b.srun("vector", lambda e: e.scalar_tensor_tensor(
                r3h(wkl)[:, :, :], r3(wkP)[:, :, :], 1.0, r3h(wkh)[:, :, :],
                op0=ALU.mult, op1=ALU.subtract), inc=ve)
            t_wk = ve.n
            b.wait("tensor", ve, t_wk)
            b.wait("tensor", sc, sc.n)
            kring = [0, 0]
            stg_free = [0]
            for tt in range(8):
                ks = tt % 2
                b.wait("tensor", sc, kring[ks])
                nmm = 0
                for c in range(8):
                    for ph in range(3):
                        stat = (r3h(wkh)[:, c, :] if ph < 2 else r3h(wkl)[:, c, :])
                        mov = (r3h(xnTh)[:, c, 512 * tt:512 * (tt + 1)] if ph != 1
                               else r3h(xnTl)[:, c, 512 * tt:512 * (tt + 1)])
                        nmm += 1
                        b.run("tensor", lambda e, stat=stat, mov=mov, nmm=nmm, ks=ks: e.matmul(
                            psB[ks][:, 0:512], stat, mov,
                            start=(nmm == 1), stop=(nmm == 24)), inc=(pe if nmm == 24 else None))
                tk = pe.n
                b.wait("scalar", pe, tk)
                b.wait("scalar", ve, stg_free[0])
                if tt == 0:
                    b.wait("scalar", ve, t_wk)
                b.run("scalar", lambda e, hp=hp, ks=ks: e.activation(
                    f32stg, psB[ks][:, 0:512], AF.Identity,
                    bias=c_bqk[:, 8 + hp:9 + hp], scale=1.0), inc=sc)
                kring[ks] = sc.n
                kb0c = kbs[0][:, 512 * tt:512 * (tt + 1)]
                kb1c = kbs[1][:, 512 * tt:512 * (tt + 1)]
                b.srun("scalar", lambda e, kb0c=kb0c: e.activation(
                    kb0c[0:64, :], f32stg[0:64, :], AF.Identity, bias=0.0, scale=1.0), inc=sc)
                t_s2 = sc.n
                b.srun("scalar", lambda e: e.activation(
                    tmpb[64:128, :], f32stg[64:128, :], AF.Identity, bias=0.0, scale=1.0), inc=sc)
                t_s3 = sc.n
                b.srun("scalar", lambda e, kb1c=kb1c: e.activation(
                    kb1c[0:64, :], tmpb[64:128, :], AF.Identity, bias=0.0, scale=1.0), inc=sc)
                t_s4 = sc.n
                b.wait("vector", sc, t_s3)
                b.run("vector", lambda e, kb1c=kb1c: e.scalar_tensor_tensor(
                    kb1c[64:128, :], f32stg[64:128, :], 1.0, tmpb[64:128, :],
                    op0=ALU.mult, op1=ALU.subtract), inc=ve)
                b.run("vector", lambda e, kb0c=kb0c: e.scalar_tensor_tensor(
                    lo0t[0:64, :], f32stg[0:64, :], 1.0, kb0c[0:64, :],
                    op0=ALU.mult, op1=ALU.subtract), inc=ve)
                b.srun("vector", lambda e, kb0c=kb0c: e.tensor_copy(
                    kb0c[64:128, :], lo0t[0:64, :]), inc=ve)
                stg_free[0] = ve.n
            t_kprod = sc.n
            t_kb_ve = ve.n

            for a in range(2):
                h = 2 * hp + a
                base = 64 * a
                b.wait("sync", gp, t_vones)
                b.wait("sync", st, t_v_done)
                b.wait("sync", pe, max(uguard_pe))
                def _(e, h=h):
                    e.dma_start(out=vh.rearrange("p (k c) -> p k c", c=65)[:, :, 0:64],
                                in_=v_hbm.rearrange("(k p) n -> p k n", p=128)[:, :, 64 * h:64 * (h + 1)]
                                ).then_inc(ld.h, 16)
                E("sync", _)
                ld.n += 16
                ld_v = ld.n
                b.wait("sync", st, qstore[0])
                def _(e, hp=hp, a=a):
                    for l in range(2):
                        dst = qhi2 if l == 0 else qlo2
                        srcq = q_hbm[l, hp, 64 * a:64 * (a + 1), :]
                        e.dma_start(out=dst[0:64, :], in_=srcq).then_inc(ld.h, 16)
                        e.dma_start(out=dst[64:128, :], in_=srcq).then_inc(ld.h, 16)
                E("sync", _)
                ld.n += 64
                ld_q = ld.n

                emc = lnst[:, 6:7]
                t_emtr = 0
                sringQsc = [0, 0]
                sqc = 0
                for i in range(NS):
                    EK = 128 * (NKC - 2 * i)
                    nkt = (EK + 511) // 512
                    for kt in range(nkt):
                        width = min(512, EK - 512 * kt)
                        sq = sqc % 2
                        sqc += 1
                        psq = psB[1].ap()[:, 512 * sq:512 * sq + width]
                        b.wait("tensor", ve, sringQ[sq])
                        b.wait("tensor", sc, sringQsc[sq])
                        if i == 0 and kt == 0:
                            b.wait("tensor", sc, t_kprod)
                            b.wait("tensor", ve, t_kb_ve)
                            b.wait("tensor", ld, ld_q)
                        kbm = kbs[a][:, 512 * kt:512 * kt + width]
                        b.run("tensor", lambda e, i=i, psq=psq, kbm=kbm: e.matmul(
                            psq, qhi2[:, 128 * i:128 * (i + 1)], kbm,
                            start=True, stop=False))
                        b.run("tensor", lambda e, i=i, psq=psq, kbm=kbm: e.matmul(
                            psq, qlo2[:, 128 * i:128 * (i + 1)], kbm,
                            start=False, stop=True), inc=pe)
                        t_mm = pe.n
                        b.wait("vector", pe, t_mm)
                        if kt == nkt - 1:
                            b.run("vector", lambda e, psq=psq, width=width: e.tensor_tensor(
                                psq[:, width - 256:width - 128],
                                psq[:, width - 256:width - 128],
                                masks[:, 256:384], op=ALU.add), inc=ve)
                            b.run("vector", lambda e, psq=psq, width=width: e.tensor_tensor(
                                psq[:, width - 128:width],
                                psq[:, width - 128:width],
                                masks[:, 384:512], op=ALU.add), inc=ve)
                        if nkt == 1:
                            b.wait("vector", sc, emc_rd[i])
                            b.srun("vector", lambda e, i=i, psq=psq: e.reduce_max(
                                mrun[:, 4 * i:4 * i + 1], psq, axis=mybir.AxisListType.X), inc=ve)
                        else:
                            b.srun("vector", lambda e, kt=kt, psq=psq: e.reduce_max(
                                mrun[:, 64 + 4 * kt:65 + 4 * kt], psq,
                                axis=mybir.AxisListType.X), inc=ve)
                        sringQ[sq] = ve.n
                    if nkt > 1:
                        b.wait("vector", sc, emc_rd[i])
                        b.srun("vector", lambda e, i=i, nkt=nkt: e.reduce_max(
                            mrun[:, 4 * i:4 * i + 1],
                            mrun.ap()[:, 64:64 + 4 * nkt].rearrange(
                                "p (a b) -> p a b", b=4)[:, :, 0:1].rearrange(
                                "p a b -> p (a b)"),
                            axis=mybir.AxisListType.X), inc=ve)
                    t_mx = ve.n
                    b.wait("scalar", ve, t_mx)
                    b.wait("scalar", pe, t_emtr)
                    b.run("scalar", lambda e, i=i: e.activation(
                        emc, mrun[:, 4 * i:4 * i + 1], AF.Exp, scale=0.125), inc=sc)
                    t_emc = sc.n
                    emc_rd[i] = sc.n
                    sq = sqc % 2
                    sqc += 1
                    b.wait("tensor", ve, sringQ[sq])
                    b.wait("tensor", sc, max(sringQsc[sq], t_emc))
                    b.run("tensor", lambda e, sq=sq: e.transpose(
                        psB[1].ap()[0:1, 512 * sq:512 * sq + 128], emc, ident[:]), inc=pe)
                    t_emtr = pe.n
                    b.wait("scalar", pe, t_emtr)
                    b.wait("scalar", ve, emrow_rd[i // 4])
                    b.run("scalar", lambda e, i=i, sq=sq: e.copy(
                        emrow[0:1, 128 * i:128 * (i + 1)],
                        psB[1].ap()[0:1, 512 * sq:512 * sq + 128]), inc=sc)
                    sringQsc[sq] = sc.n
                t_em = sc.n

                svstart = [True] * NW
                t_svw = [0] * NW
                t_expk = {}
                pS_map = {}
                sslot = [0]

                def emit_scores(k, a=a, h=h):
                    P = Pfn(k)
                    pref = 128 * P
                    nwin = (P + 3) // 4
                    ub = k % 2
                    for w in range(nwin):
                        width = min(512, pref - 512 * w)
                        slot = sslot[0] % 2
                        sslot[0] += 1
                        pS = psB[0].ap()[:, 512 * slot:512 * slot + width]
                        pS_map[(k, w)] = pS
                        b.wait("tensor", sc, sringS[slot])
                        kbm = kbs[a][:, 128 * k:128 * (k + 1)]
                        b.run("tensor", lambda e, pS=pS, w=w, width=width, kbm=kbm: e.matmul(
                            pS, kbm, qhi2[:, 512 * w:512 * w + width],
                            start=True, stop=False))
                        b.run("tensor", lambda e, pS=pS, w=w, width=width, kbm=kbm: e.matmul(
                            pS, kbm, qlo2[:, 512 * w:512 * w + width],
                            start=False, stop=True), inc=pe)
                        t_mm = pe.n
                        lastw = (P - 1) // 4
                        if w == lastw:
                            off = (pref - 128) - 512 * w
                            mcol = 128 * (k % 2)
                            b.wait("vector", pe, t_mm)
                            b.run("vector", lambda e, off=off, mcol=mcol, pS=pS: e.tensor_tensor(
                                pS[:, off:off + 128], pS[:, off:off + 128],
                                masks[:, mcol:mcol + 128], op=ALU.add), inc=ve)
                            b.wait("scalar", ve, ve.n)
                        else:
                            b.wait("scalar", pe, t_mm)
                        b.wait("scalar", pe, uguard_pe[ub])
                        b.run("scalar", lambda e, w=w, width=width, pS=pS, ub=ub: e.activation(
                            ubuf[ub][:, 512 * w:512 * w + width], pS, AF.Exp, scale=0.125), inc=sc)
                        sringS[slot] = sc.n
                        t_expk[(k, w)] = sc.n

                def emit_sv(k, a=a, h=h):
                    P = Pfn(k)
                    pref = 128 * P
                    nwin = (P + 3) // 4
                    ub = k % 2
                    for w in range(nwin):
                        width = min(512, pref - 512 * w)
                        b.wait("tensor", sc, t_expk[(k, w)])
                        if k == 0 and w == 0:
                            b.wait("tensor", ld, ld_v)
                        if svstart[w]:
                            b.wait("tensor", ve, t_aev[w])
                        b.run("tensor", lambda e, k=k, w=w, width=width, s=svstart[w], ub=ub: e.matmul(
                            psA[w][0:65, 0:width], vh[:, 65 * k:65 * (k + 1)],
                            ubuf[ub][:, 512 * w:512 * w + width],
                            start=s, stop=(k == KEND[w])), inc=pe)
                        svstart[w] = False
                        t_svw[w] = pe.n
                    uguard_pe[ub] = pe.n

                def emit_deaths(k, a=a, h=h):
                    P = Pfn(k)
                    nwin = (P + 3) // 4
                    for w in range(nwin):
                        if k == KEND[w]:
                            b.wait("vector", pe, t_svw[w])
                            b.wait("vector", st, st.n)
                            b.wait("vector", ve, asb_free[0])
                            b.run("vector", lambda e, w=w: e.tensor_copy(
                                asb65, psA[w][0:65, 0:512]), inc=ve)
                            t_aev[w] = ve.n
                            sq = (sslot[0] + 1) % 2
                            b.wait("tensor", ve, max(sringQ[sq], t_aev[w]))
                            b.run("tensor", lambda e, sq=sq: e.matmul(
                                psB[1].ap()[0:1, 512 * sq:512 * sq + 512],
                                unitv[0:65, 0:1], asb65[0:65, :],
                                start=True, stop=True), inc=pe)
                            t_ur = pe.n
                            b.wait("vector", pe, t_ur)
                            b.wait("vector", sc, t_em)
                            b.srun("vector", lambda e, h=h, w=w, sq=sq: e.scalar_tensor_tensor(
                                denb, emrow[0:1, 512 * w:512 * (w + 1)],
                                c_gh[0:1, h:h + 1],
                                psB[1].ap()[0:1, 512 * sq:512 * sq + 512],
                                op0=ALU.mult, op1=ALU.add), inc=ve)
                            emrow_rd[w] = ve.n
                            b.srun("vector", lambda e: e.reciprocal(denb, denb), inc=ve)
                            t_den = ve.n
                            b.wait("tensor", ve, t_den)
                            b.run("tensor", lambda e, sq=sq: e.matmul(
                                psB[1].ap()[0:64, 512 * sq:512 * sq + 512],
                                ones1[0:1, 0:64], denb,
                                start=True, stop=True), inc=pe)
                            t_rb = pe.n
                            b.wait("vector", pe, t_rb)
                            b.srun("vector", lambda e, sq=sq: e.tensor_tensor(
                                asb65[0:64, :], asb65[0:64, :],
                                psB[1].ap()[0:64, 512 * sq:512 * sq + 512], op=ALU.mult), inc=ve)
                            asb_free[0] = ve.n
                            sringQ[sq] = ve.n
                            b.wait("sync", ve, asb_free[0])
                            def _(e, w=w, h=h):
                                e.dma_start(out=a_hbm[h // 2, 64 * (h % 2):64 * (h % 2) + 64,
                                                      512 * w:512 * (w + 1)],
                                            in_=asb65[0:64, :]).then_inc(st.h, 16)
                            E("sync", _)
                            st.n += 16

                emit_scores(0)
                for k in range(NKC):
                    if k + 1 < NKC:
                        emit_scores(k + 1)
                    emit_sv(k)
                    emit_deaths(k)
            if stop == "H1" and hp == 0:
                return finish()
        t_attn_done = st.n

        def _(e):
            e.dma_start(out=c_cA[:], in_=sb2[:]).then_inc(ld.h, 16)
            e.dma_start(out=c_b2t[:, 0:8], in_=b2e[0]).then_inc(ld.h, 16)
            e.dma_start(out=c_b2t[:, 8:16], in_=b2e[1]).then_inc(ld.h, 16)
            e.dma_start(out=gdif, in_=g_hbm[:]).then_inc(ld.h, 16)
        E("sync", _)
        ld.n += 64
        ld_b2 = ld.n
        b.wait("sync", st, t_attn_done)
        b.wait("gpsimd", st, t_attn_done)
        b.wait("vector", ld, ld_b2)
        b.srun("vector", lambda e: e.tensor_tensor(c_cA[:], c_cA[:], c_b2t[:, 0:8], op=ALU.add), inc=ve)
        b.srun("vector", lambda e: e.tensor_tensor(c_cA[:], c_cA[:], c_b2t[:, 8:16], op=ALU.add), inc=ve)
        b.srun("vector", lambda e: e.tensor_scalar(c_rb1[:], c_b1e[:], 0.0, 0.0,
                                                   op0=ALU.max, op1=ALU.add), inc=ve)
        mstg = big3.ap()[0:1, 0:2048]
        b.run("vector", lambda e: e.tensor_scalar(
            mstg, gdif, 0.0, 1.0, op0=ALU.is_gt, op1=ALU.mult), inc=ve)
        t_gdif_read = ve.n
        b.wait("tensor", ve, t_gdif_read)
        for cc in range(4):
            b.run("tensor", lambda e, cc=cc: e.matmul(
                psB[cc % 2][:, 0:512], ones1[:],
                mstg[0:1, 512 * cc:512 * (cc + 1)],
                start=True, stop=True), inc=pe)
            t_b = pe.n
            b.wait("vector", pe, t_b)
            b.run("vector", lambda e, cc=cc: e.tensor_copy(
                m1b[:, 512 * cc:512 * (cc + 1)], psB[cc % 2][:, 0:512]), inc=ve)
        t_mask_b = ve.n

        paths = [(w1s, w2s, c_b1s.ap(), None),
                 (w1e0, w2e0, c_b1e.ap()[:, 0:32], (c_rb1.ap()[:, 0:32], True)),
                 (w1e1, w2e1, c_b1e.ap()[:, 32:64], (c_rb1.ap()[:, 32:64], False))]
        CH = [(ip, hc) for ip in range(3) for hc in range(8)]

        wfree = [0, 0]
        w2free = [0, 0]
        H_last = [t_gdif_read, 0]
        xfree = [0, 0]
        ofring = [0, 0]
        hring = [t_mask_b, t_mask_b]
        first_acc = True
        sc_w1 = {}; ve_w1 = {}; sc_w2 = {}; ve_w2 = {}
        pend = {}

        def piece_views(cidx, j):
            wb = cidx % 2
            ip, hc = CH[cidx]
            w1x, w2x = paths[ip][0], paths[ip][1]
            if j < 2:
                hi = w1c[wb][0][:, 2048 * j:2048 * (j + 1)]
                lo = w1c[wb][1][:, 2048 * j:2048 * (j + 1)]
                src = w1x.rearrange("(c p) n -> p c n", p=128)[
                    :, 4 * j:4 * (j + 1), 512 * hc:512 * (hc + 1)]
                a = 4
            else:
                s_ = j - 2
                hi = w2c[wb][0][:, 2048 * s_:2048 * (s_ + 1)]
                lo = w2c[wb][1][:, 2048 * s_:2048 * (s_ + 1)]
                src = w2x.rearrange("(hc s p) n -> hc p s n", p=128, s=4)[hc][
                    :, 2 * s_:2 * (s_ + 1), :]
                a = 2
            return hi, lo, src, a

        def emit_dma_cast(cidx, j):
            wb = cidx % 2
            hi, lo, src, a = piece_views(cidx, j)
            h_ = j % 2
            b.wait("sync", ve, H_last[h_])
            def _(e, src=src, h_=h_, a=a):
                e.dma_start(out=Hbuf[h_].rearrange("p (a n) -> p a n", a=a),
                            in_=src).then_inc(ld.h, 16)
            E("sync", _)
            ld.n += 16
            b.wait("scalar", ld, ld.n)
            b.wait("scalar", pe, (wfree[wb] if j < 2 else w2free[wb]))
            b.run("scalar", lambda e, hi=hi, h_=h_: e.activation(
                hi, Hbuf[h_], AF.Identity, bias=0.0, scale=1.0), inc=sc)
            pend[(cidx, j)] = sc.n

        def emit_stt(cidx, j):
            wb = cidx % 2
            hi, lo, src, a = piece_views(cidx, j)
            h_ = j % 2
            b.wait("vector", sc, pend[(cidx, j)])
            b.wait("vector", pe, (wfree[wb] if j < 2 else w2free[wb]))
            b.run("vector", lambda e, hi=hi, lo=lo, h_=h_: e.scalar_tensor_tensor(
                lo, Hbuf[h_], 1.0, hi, op0=ALU.mult, op1=ALU.subtract), inc=ve)
            H_last[h_] = ve.n
            if j == 1:
                sc_w1[cidx] = pend[(cidx, 1)]; ve_w1[cidx] = ve.n
            if j == 3:
                sc_w2[cidx] = pend[(cidx, 3)]; ve_w2[cidx] = ve.n

        for j in range(4):
            emit_dma_cast(0, j)
            emit_stt(0, j)

        for cidx, (ip, hc) in enumerate(CH):
            wb_ = cidx % 2
            w1x, w2x, b1ap, maskinfo = paths[ip]
            nxt = cidx + 1 < len(CH)
            if nxt:
                emit_dma_cast(cidx + 1, 0)
                emit_dma_cast(cidx + 1, 1)
            for t in range(4):
                xb_ = (4 * cidx + t) % 2
                b.wait("gpsimd", pe, xfree[xb_])
                def _(e, t=t, xb_=xb_):
                    e.dma_start(out=r3h(xot[xb_][0])[:, :, :],
                                in_=xt_hbm.rearrange("l c p n -> p l c n")[:, 0, :, 512 * t:512 * (t + 1)]
                                ).then_inc(lx.h, 16)
                    e.dma_start(out=r3h(xot[xb_][1])[:, :, :],
                                in_=xt_hbm.rearrange("l c p n -> p l c n")[:, 1, :, 512 * t:512 * (t + 1)]
                                ).then_inc(lx.h, 16)
                E("gpsimd", _)
                lx.n += 32
                ld_x = lx.n
                if t == 0:
                    b.wait("tensor", ve, ve_w1[cidx])
                    b.wait("tensor", sc, sc_w1[cidx])
                b.wait("tensor", lx, ld_x)
                for su in range(4):
                    hs = su % 2
                    b.wait("tensor", ve, hring[hs])
                    nmm = 0
                    for c in range(8):
                        for ph in range(3):
                            stat = (r3h(w1c[wb_][0])[:, c, 128 * su:128 * (su + 1)] if ph < 2
                                    else r3h(w1c[wb_][1])[:, c, 128 * su:128 * (su + 1)])
                            mov = (r3h(xot[xb_][0])[:, c, :] if ph != 1
                                   else r3h(xot[xb_][1])[:, c, :])
                            nmm += 1
                            b.run("tensor", lambda e, stat=stat, mov=mov, nmm=nmm, hs=hs: e.matmul(
                                psB[hs][:, 0:512], stat, mov,
                                start=(nmm == 1), stop=(nmm == 24)), inc=(pe if nmm == 24 else None))
                    th = pe.n
                    b.wait("vector", pe, th)
                    if maskinfo is None:
                        b.srun("vector", lambda e, hs=hs, b1ap=b1ap, hc=hc, su=su: e.tensor_scalar(
                            hself, psB[hs][:, 0:512],
                            b1ap[:, 4 * hc + su:4 * hc + su + 1], 0.0,
                            op0=ALU.add, op1=ALU.max), inc=ve)
                    else:
                        rb1ap, swap = maskinfo
                        b.wait("vector", ve, t_mask_b)
                        b.srun("vector", lambda e, hs=hs, b1ap=b1ap, hc=hc, su=su: e.tensor_scalar(
                            relscr, psB[hs][:, 0:512],
                            b1ap[:, 4 * hc + su:4 * hc + su + 1], 0.0,
                            op0=ALU.add, op1=ALU.max), inc=ve)
                        if swap:
                            b.srun("vector", lambda e, su=su, t=t, rb1ap=rb1ap, hc=hc: e.select(
                                hself, m1b[:, 512 * t:512 * (t + 1)],
                                rb1ap[:, 4 * hc + su:4 * hc + su + 1].to_broadcast((128, 512)),
                                relscr), inc=ve)
                        else:
                            b.srun("vector", lambda e, su=su, t=t, rb1ap=rb1ap, hc=hc: e.select(
                                hself, m1b[:, 512 * t:512 * (t + 1)], relscr,
                                rb1ap[:, 4 * hc + su:4 * hc + su + 1].to_broadcast((128, 512))),
                                inc=ve)
                    b.srun("vector", lambda e, su=su: e.tensor_copy(
                        r3h(hbh, 4)[:, su, :], hself), inc=ve)
                    b.srun("vector", lambda e, su=su: e.scalar_tensor_tensor(
                        r3h(hbl, 4)[:, su, :], hself, 1.0, r3h(hbh, 4)[:, su, :],
                        op0=ALU.mult, op1=ALU.subtract), inc=ve)
                    hring[hs] = ve.n
                t_h = ve.n
                b.wait("tensor", ve, t_h)
                if t == 0:
                    b.wait("tensor", ve, ve_w2[cidx])
                    b.wait("tensor", sc, sc_w2[cidx])
                for of in range(8):
                    os_ = of % 2
                    b.wait("tensor", ve, ofring[os_])
                    nmm = 0
                    for su in range(4):
                        for ph in range(3):
                            stat = (r3h(w2c[wb_][0], 4)[:, su, 128 * of:128 * (of + 1)] if ph < 2
                                    else r3h(w2c[wb_][1], 4)[:, su, 128 * of:128 * (of + 1)])
                            mov = (r3h(hbh, 4)[:, su, :] if ph != 1 else r3h(hbl, 4)[:, su, :])
                            nmm += 1
                            b.run("tensor", lambda e, stat=stat, mov=mov, nmm=nmm, os_=os_: e.matmul(
                                psA[os_][:, 0:512], stat, mov,
                                start=(nmm == 1), stop=(nmm == 12)), inc=(pe if nmm == 12 else None))
                    tw2 = pe.n
                    b.wait("vector", pe, tw2)
                    accs = r3(ACC)[:, of, 512 * t:512 * (t + 1)]
                    if first_acc:
                        b.run("vector", lambda e, accs=accs, os_=os_: e.tensor_copy(
                            accs, psA[os_][:, 0:512]), inc=ve)
                    else:
                        b.run("vector", lambda e, accs=accs, os_=os_: e.scalar_tensor_tensor(
                            accs, psA[os_][:, 0:512], 1.0, accs,
                            op0=ALU.mult, op1=ALU.add), inc=ve)
                    ofring[os_] = ve.n
                xfree[xb_] = pe.n
                if nxt:
                    if t == 0:
                        emit_stt(cidx + 1, 0)
                    elif t == 1:
                        emit_stt(cidx + 1, 1)
                        emit_dma_cast(cidx + 1, 2)
                    elif t == 2:
                        emit_stt(cidx + 1, 2)
                        emit_dma_cast(cidx + 1, 3)
                    else:
                        emit_stt(cidx + 1, 3)
            if cidx == 0:
                first_acc = False
            wfree[wb_] = pe.n
            w2free[wb_] = pe.n
        t_ffn = ve.n

        b.wait("sync", pe, max(wfree + w2free))
        def _(e):
            e.dma_start(out=woA.rearrange("p (c n) -> p c n", c=8),
                        in_=woT.rearrange("(c p) n -> p c n", p=128)).then_inc(ld.h, 16)
        E("sync", _)
        ld.n += 16
        ld_wo = ld.n
        sg_done = [0]

        woring = [0, 0]
        ost = [0]
        for t in range(4):
            ab_ = t % 2
            b.wait("sync", pe, xfree[ab_])
            def _(e, t=t, ab_=ab_):
                e.dma_start(out=r3(atile[ab_])[:, :, :],
                            in_=a_hbm.rearrange("c p n -> p c n")[:, :, 512 * t:512 * (t + 1)]
                            ).then_inc(ld.h, 16)
            E("sync", _)
            ld.n += 16
            ld_a = ld.n
            b.wait("tensor", ld, ld_a)
            b.wait("tensor", ld, ld_wo)
            if t == 0:
                b.wait("tensor", ve, t_ffn)
                b.wait("tensor", st, t_attn_done)
            for of in range(8):
                ws_ = of % 2
                b.wait("tensor", sc, woring[ws_])
                for c in range(8):
                    wsrc = woA.rearrange("p (cc n) -> p cc n", cc=8)[:, c, 128 * of:128 * (of + 1)]
                    b.run("tensor", lambda e, wsrc=wsrc, ws_=ws_, ab_=ab_, c=c: e.matmul(
                        psB[ws_][:, 0:512], wsrc, r3(atile[ab_])[:, c, :],
                        start=(c == 0), stop=(c == 7)), inc=(pe if c == 7 else None))
                ts_ = pe.n
                b.wait("scalar", pe, ts_)
                b.wait("scalar", ve, sg_done[0])
                b.run("scalar", lambda e, of=of, ws_=ws_: e.activation(
                    stile, psB[ws_][:, 0:512], AF.Identity, bias=c_bo[:, of:of + 1],
                    scale=1.0), inc=sc)
                woring[ws_] = sc.n
                b.wait("scalar", ve, t_ffn)
                b.run("scalar", lambda e, of=of, t=t: e.activation(
                    gtile, r3(ACC)[:, of, 512 * t:512 * (t + 1)], AF.Silu,
                    bias=c_cA[:, of:of + 1], scale=1.0), inc=sc)
                tg = sc.n
                b.wait("sync", ve, ve.n)
                def _(e, of=of, t=t):
                    e.dma_start(out=xt_in, in_=x_ownT[of, :, 512 * t:512 * (t + 1)]
                                ).then_inc(ld.h, 16)
                E("sync", _)
                ld.n += 16
                ld_xt = ld.n
                b.wait("vector", sc, tg)
                b.run("vector", lambda e: e.tensor_tensor(sgt, stile, gtile, op=ALU.mult), inc=ve)
                sg_done[0] = ve.n
                b.wait("vector", ld, ld_xt)
                b.wait("vector", st, ost[0])
                b.srun("vector", lambda e: e.tensor_tensor(obuf, sgt, xt_in, op=ALU.add), inc=ve)
                t_o = ve.n
                b.wait("sync", ve, t_o)
                def _(e, of=of, t=t):
                    e.dma_start(out=out[of, :, 512 * t:512 * (t + 1)], in_=obuf
                                ).then_inc(st.h, 16)
                E("sync", _)
                st.n += 16
                ost[0] = st.n
            xfree[ab_] = pe.n
        b.wait("sync", st, st.n)
        return finish()
    return nc


def _prep_core(inputs, bseq, p):
    x = np.ascontiguousarray(np.asarray(inputs["L0_attn_inputs"][bseq], dtype=np.float32))
    qidx = [31 - 2 * i - p for i in range(NS)]
    rows = np.concatenate([np.arange(128 * qi, 128 * qi + 128) for qi in qidx])
    x_own = np.ascontiguousarray(x[rows])
    x_ownT = np.ascontiguousarray(x_own.T.reshape(8, 128, TOK))
    mk = lambda a: np.ascontiguousarray(np.asarray(a, dtype=np.float32))
    dt = np.broadcast_to(np.array([[128.0 * (p - 1), 128.0 * p]], np.float32), (128, 2))
    return {
        "x_seq": x, "x_own": x_own, "x_ownT": x_ownT,
        "wqkvT": mk(inputs["L0_attn_Wqkv_w"].T),
        "bqk": mk(np.asarray(inputs["L0_attn_Wqkv_b"])[0:2048].reshape(16, 128).T),
        "bv": mk(np.broadcast_to(np.asarray(inputs["L0_attn_Wqkv_b"])[2048:3072].reshape(1, DM), (128, DM))),
        "woT": mk(inputs["L0_attn_Wo_w"].T),
        "bo": mk(np.asarray(inputs["L0_attn_Wo_b"]).reshape(8, 128).T),
        "ghost": mk(np.asarray(inputs["L0_attn_ghost"]).reshape(1, H)),
        "gateT": mk(np.asarray(inputs["L0_ffn_gate_w"]).T.reshape(8, 128, 2)),
        "w1s": mk(inputs["L0_ffn_sw1"]),
        "b1s": mk(np.asarray(inputs["L0_ffn_sb1"]).reshape(32, 128).T),
        "w2s": mk(inputs["L0_ffn_sw2"]),
        "sb2": mk(np.asarray(inputs["L0_ffn_sb2"]).reshape(8, 128).T),
        "w1e0": mk(inputs["L0_ffn_w1"][0]), "w1e1": mk(inputs["L0_ffn_w1"][1]),
        "b1e": mk(np.asarray(inputs["L0_ffn_b1"]).reshape(2, 32, 128).transpose(0, 2, 1)),
        "w2e0": mk(inputs["L0_ffn_w2"][0]), "w2e1": mk(inputs["L0_ffn_w2"][1]),
        "b2e": mk(np.asarray(inputs["L0_ffn_b2"]).reshape(2, 8, 128).transpose(0, 2, 1)),
        "dtab": dt,
    }


def kernel(**inputs):
    nc = build_nc()
    in_maps = []
    for core in range(8):
        bseq, p = core // 2, core % 2
        in_maps.append(_prep_core(inputs, bseq, p))
    res = run_bass_kernel_spmd(nc, in_maps, core_ids=list(range(8)))
    out = np.empty((4, S, DM), np.float32)
    for core in range(8):
        bseq, p = core // 2, core % 2
        shard = np.asarray(res.results[core]["out"])
        own = shard.reshape(DM, TOK).T
        for i in range(NS):
            qi = 31 - 2 * i - p
            out[bseq, 128 * qi:128 * qi + 128] = own[128 * i:128 * (i + 1)]
    return out


# version 18
# speedup vs baseline: 1.0565x; 1.0565x over previous
import numpy as np
from contextlib import ExitStack

import concourse.bass as bass
import concourse.mybir as mybir
from concourse.bass_utils import run_bass_kernel_spmd
from concourse.bass_isa import ReduceOp
from concourse import library_config

F32 = mybir.dt.float32
BF16 = mybir.dt.bfloat16
AF = mybir.ActivationFunctionType
ALU = mybir.AluOpType

H, HD, DM, S = 16, 64, 1024, 4096
NS, SC = 16, 128
TOK = NS * SC
NKC = S // 128
NW = 4
HID = 4096
NEG = -1.0e30


def Pfn(k):
    return (NKC - k + 1) // 2


KEND = [NKC - 1 - 8 * w for w in range(NW)]

ENGS = ("sync", "tensor", "vector", "scalar", "gpsimd")


class Sem:
    def __init__(self, nc, name):
        self._cm = nc.semaphore(name)
        self.h = self._cm.__enter__()
        self.n = 0

    def inc(self, k=1):
        self.n += k
        return self.n


class Builder:
    def __init__(self, nc):
        self.nc = nc
        self.ops = {e: [] for e in ENGS}

    def emit(self, eng, fn):
        self.ops[eng].append(fn)

    def wait(self, eng, sem, val):
        if val <= 0:
            return
        self.emit(eng, lambda e, s=sem.h, v=val: e.wait_ge(s, v))

    def run(self, eng, fn, inc=None, inc_by=1):
        if inc is None:
            self.emit(eng, fn)
            return None
        t = inc.inc(inc_by)
        self.emit(eng, lambda e, f=fn, s=inc.h, k=inc_by: f(e).then_inc(s, k))
        return t

    def srun(self, eng, fn, inc):
        self.wait(eng, inc, inc.n)
        return self.run(eng, fn, inc=inc)


def build_nc(stop=None, dbg=False):
    nc = bass.Bass("TRN2", target_bir_lowering=False, debug=False,
                   detect_race_conditions=False)
    dti = lambda name, shape: nc.dram_tensor(name, shape, F32, kind="ExternalInput").ap()
    x_seq = dti("x_seq", [S, DM])
    x_own = dti("x_own", [TOK, DM])
    x_ownT = dti("x_ownT", [8, 128, TOK])
    wqkvT = dti("wqkvT", [DM, 3 * DM])
    bqk = dti("bqk", [128, 16])
    bv = dti("bv", [128, DM])
    woT = dti("woT", [DM, DM])
    bo = dti("bo", [128, 8])
    ghost = dti("ghost", [1, H])
    gateT = dti("gateT", [8, 128, 2])
    w1s = dti("w1s", [DM, HID])
    b1s = dti("b1s", [128, 32])
    w2s = dti("w2s", [HID, DM])
    sb2 = dti("sb2", [128, 8])
    w1e0 = dti("w1e0", [DM, HID])
    w1e1 = dti("w1e1", [DM, HID])
    b1e = dti("b1e", [2, 128, 32])
    w2e0 = dti("w2e0", [HID, DM])
    w2e1 = dti("w2e1", [HID, DM])
    b2e = dti("b2e", [2, 128, 8])
    dtab = dti("dtab", [128, 2])
    out = nc.dram_tensor("out", [8, 128, TOK], F32, kind="ExternalOutput").ap()
    dts = lambda name, shape: nc.dram_tensor(name, shape, F32, kind="ExternalOutput" if dbg else "Internal").ap()
    q_hbm = nc.dram_tensor("q_hbm", [2, 8, 128, TOK], BF16,
                           kind="ExternalOutput" if dbg else "Internal").ap()
    xt_hbm = nc.dram_tensor("xt_hbm", [2, 8, 128, TOK], BF16, kind="ExternalOutput" if dbg else "Internal").ap()
    v_hbm = dts("v_hbm", [S, DM])
    a_hbm = dts("a_hbm", [8, 128, TOK])
    g_hbm = dts("g_hbm", [1, TOK])

    b = Builder(nc)
    with ExitStack() as ctx:
        sbuf = lambda name, shape, dt=F32: ctx.enter_context(nc.sbuf_tensor(name, shape, dt))
        psm = lambda name, shape: ctx.enter_context(nc.psum_tensor(name, shape, F32))
        sem = lambda name: Sem(nc, name)

        ld = sem("ld")
        st = sem("st")
        cp = sem("cp")
        lx = sem("lx")
        pe = sem("pe")
        sc = sem("sc")
        ve = sem("ve")
        gp = sem("gp")

        psA = [psm(f"psA{i}", [128, 512]) for i in range(4)]
        psB = [psm(f"psB{i}", [128, 1024]) for i in range(2)]

        iotB = sbuf("iotB", [128, 128])
        ident = sbuf("ident", [128, 128])
        masks = sbuf("masks", [128, 512], BF16)
        c_dt = sbuf("c_dt", [128, 2])
        c_bqk = sbuf("c_bqk", [128, 16])
        c_bo = sbuf("c_bo", [128, 8])
        c_gh = sbuf("c_gh", [1, H])
        c_b1s = sbuf("c_b1s", [128, 32])
        c_b1e = sbuf("c_b1e", [128, 64])
        c_rb1 = sbuf("c_rb1", [128, 64])
        c_cA = sbuf("c_cA", [128, 8])
        c_b2t = sbuf("c_b2t", [128, 16])
        lnst = sbuf("lnst", [128, 12])
        gtsb = sbuf("gtsb", [128, 16])
        gdw = sbuf("gdw", [128, 8])
        gdwb = sbuf("gdwb", [128, 16], BF16)
        ones1 = sbuf("ones1", [1, 128])
        unitv = sbuf("unitv", [128, 1])
        mrun = sbuf("mrun", [128, 104])

        xnTb = sbuf("xnTb", [128, 32768])
        big1 = sbuf("big1", [128, 8192])
        big2 = sbuf("big2", [128, 8192])
        big3 = sbuf("big3", [128, 2048])

        A = lambda t, o, n: t.ap()[:, o:o + n]
        Ab = lambda t, o, n: t.ap()[:, o:o + n].bitcast(BF16)
        xnTh = Ab(xnTb, 0, 16384)
        xnTl = Ab(xnTb, 16384, 16384)
        wbig = A(big1, 0, 2048)
        wbh = Ab(big1, 2048, 1024)
        wbl = Ab(big1, 3072, 1024)
        c_bvb = A(big1, 6144, 1024)
        qth = Ab(big1, 4096, 256)
        qtl = Ab(big1, 4352, 256)
        sqscr = A(big1, 7168, 1024)
        wvstg = A(big2, 0, 4096)
        wvh = Ab(big2, 4096, 2048)
        wvl = Ab(big2, 6144, 2048)
        xint = A(big2, 0, 2048).rearrange("p (b n) -> p b n", b=2)
        xn128 = A(big2, 2048, 2048).rearrange("p (b n) -> p b n", b=2)
        xnth = Ab(big2, 4096, 2048)
        xntl = Ab(big2, 6144, 2048)
        qtile = A(big3, 0, 512)
        vtile = A(big3, 512, 1024)
        kbs = [Ab(big1, 0, 2048), Ab(big1, 2048, 2048)]
        ubuf = [A(big1, 4096, 2048), A(big1, 6144, 2048)]
        qhi2 = Ab(big2, 0, 1024)
        qlo2 = Ab(big2, 1024, 1024)
        vh = A(big2, 2048, 2080)
        wkP = A(big2, 4128, 1024)
        f32stg = A(big2, 4128, 512)
        tmpb = Ab(big2, 4640, 256)
        lo0t = Ab(big2, 4896, 256)
        wkh = Ab(big2, 5152, 512)
        wkl = Ab(big2, 5664, 512)
        denb = big2.ap()[0:1, 6176:6688]
        asb65 = big2.ap()[0:65, 6688:7200]
        urow = big2.ap()[0:1, 7200:7712]
        emrow = big3.ap()[0:1, 0:2048]
        ACC = A(xnTb, 0, 16384)
        w1c = [(Ab(xnTb, 16384, 2048), Ab(xnTb, 18432, 2048)),
               (Ab(xnTb, 20480, 2048), Ab(xnTb, 22528, 2048))]
        w2c = [(Ab(xnTb, 24576, 2048), Ab(xnTb, 26624, 2048)),
               (Ab(xnTb, 28672, 2048), Ab(xnTb, 30720, 2048))]
        woA = A(xnTb, 16384, 8192)
        xot = [(Ab(big2, 0, 2048), Ab(big2, 2048, 2048)),
               (Ab(big2, 4096, 2048), Ab(big2, 6144, 2048))]
        atile = [A(big2, 0, 4096), A(big2, 4096, 4096)]
        hbh = Ab(big3, 0, 1024)
        hbl = Ab(big3, 1024, 1024)
        sgt = A(big1, 0, 512)
        obuf = A(big1, 512, 512)
        xt_in = A(big1, 1024, 512)
        gdif = big1.ap()[0:1, 1536:3584]
        gdtmp = big3.ap()[0:1, 1536:2048]
        I32 = mybir.dt.int32
        m1b = A(big1, 3584, 2048).bitcast(I32)
        Hbuf = [A(big1, 1536, 2048), A(big1, 5632, 2048)]
        relscr = A(big1, 7680, 512)
        hself = A(big1, 0, 512)
        stile = A(big3, 0, 512)
        gtile = A(big3, 512, 512)

        E = b.emit
        r3 = lambda ap, c=8: ap.rearrange("p (c n) -> p c n", c=c)

        def _(e):
            e.dma_start(out=c_bqk[:], in_=bqk[:]).then_inc(ld.h, 16)
            e.dma_start(out=c_bo[:], in_=bo[:]).then_inc(ld.h, 16)
            e.dma_start(out=c_gh[:], in_=ghost[:]).then_inc(ld.h, 16)
            e.dma_start(out=c_b1s[:], in_=b1s[:]).then_inc(ld.h, 16)
            e.dma_start(out=c_b1e[:, 0:32], in_=b1e[0]).then_inc(ld.h, 16)
            e.dma_start(out=c_b1e[:, 32:64], in_=b1e[1]).then_inc(ld.h, 16)
            e.dma_start(out=gtsb.ap().rearrange("p (c e) -> p c e", e=2), in_=gateT.rearrange("c p e -> p c e")).then_inc(ld.h, 16)
            e.dma_start(out=c_dt[:], in_=dtab[:]).then_inc(ld.h, 16)
            e.dma_start(out=c_bvb, in_=bv[:]).then_inc(ld.h, 16)
        E("sync", _)
        ld.n += 16 * 9
        ld_const = ld.n

        b.run("gpsimd", lambda e: e.iota(iotB[:], pattern=[[1, 128]], base=0,
                                         channel_multiplier=-1,
                                         allow_small_or_imprecise_dtypes=True), inc=gp)
        t_iot = gp.n
        b.run("gpsimd", lambda e: e.memset(ones1[:], 1.0), inc=gp)
        b.run("gpsimd", lambda e: e.memset(unitv[:], 0.0), inc=gp)
        b.wait("gpsimd", gp, gp.n)
        b.run("gpsimd", lambda e: e.memset(unitv[64:65, :], 1.0), inc=gp)
        t_gpinit = gp.n

        b.wait("vector", gp, t_iot)
        b.run("vector", lambda e: e.tensor_scalar(ident[:], iotB[:], 0.0, 1.0,
                                                  op0=ALU.is_equal, op1=ALU.mult), inc=ve)
        t_ident = ve.n
        b.wait("vector", ld, ld_const)
        b.run("vector", lambda e: e.tensor_tensor(
            gdw[:], gtsb.ap().rearrange("p (c e) -> p c e", e=2)[:, :, 1],
            gtsb.ap().rearrange("p (c e) -> p c e", e=2)[:, :, 0],
            op=ALU.subtract), inc=ve)
        t_gdw = ve.n
        b.wait("vector", ld, ld_const)
        for par in range(2):
            b.run("vector", lambda e, par=par: e.tensor_scalar(
                masks[:, 128 * par:128 * (par + 1)], iotB[:],
                c_dt[:, par:par + 1], NEG, op0=ALU.is_lt, op1=ALU.mult), inc=ve)
        for par in range(2):
            b.srun("vector", lambda e, par=par: e.tensor_scalar(
                lnst[:, 8:9], c_dt[:, par:par + 1], -1.0, 0.0,
                op0=ALU.mult, op1=ALU.add), inc=ve)
            b.srun("vector", lambda e, par=par: e.tensor_scalar(
                masks[:, 128 * (2 + par):128 * (3 + par)], iotB[:],
                lnst[:, 8:9], NEG, op0=ALU.is_gt, op1=ALU.mult), inc=ve)
        t_masks = ve.n

        def emit_ln(xin_ap, dbuf):
            s_ = lnst[:, 0:1]; mcol = lnst[:, 1:2]; vvcol = lnst[:, 2:3]
            ycol = lnst[:, 3:4]; t1 = lnst[:, 4:5]
            b.srun("vector", lambda e: e.reduce_sum(s_, xin_ap, axis=mybir.AxisListType.X), inc=ve)
            b.run("scalar", lambda e: e.activation(sqscr, xin_ap, AF.Square,
                                                   accum_out=vvcol), inc=sc)
            t_sq = sc.n
            b.wait("vector", sc, t_sq)
            b.srun("vector", lambda e: e.tensor_scalar(mcol, s_, 1.0 / DM, 0.0,
                                                       op0=ALU.mult, op1=ALU.add), inc=ve)
            b.run("vector", lambda e: e.tensor_scalar(vvcol, vvcol, 1.0 / DM, 0.0,
                                                      op0=ALU.mult, op1=ALU.add), inc=ve)
            b.srun("vector", lambda e: e.tensor_tensor(t1, mcol, mcol, op=ALU.mult), inc=ve)
            b.srun("vector", lambda e: e.tensor_tensor(vvcol, vvcol, t1, op=ALU.subtract), inc=ve)
            b.srun("vector", lambda e: e.tensor_scalar(vvcol, vvcol, 1.0e-5, 0.0,
                                                       op0=ALU.add, op1=ALU.add), inc=ve)
            tv = ve.n
            b.wait("scalar", ve, tv)
            b.wait("scalar", sc, sc.n)
            b.run("scalar", lambda e: e.sqrt(ycol, vvcol), inc=sc)
            b.wait("vector", sc, sc.n)
            b.srun("vector", lambda e: e.reciprocal(ycol, ycol), inc=ve)
            b.srun("vector", lambda e: e.tensor_tensor(t1, ycol, ycol, op=ALU.mult), inc=ve)
            b.srun("vector", lambda e: e.tensor_tensor(t1, t1, vvcol, op=ALU.mult), inc=ve)
            b.srun("vector", lambda e: e.tensor_scalar(t1, t1, -0.5, 1.5,
                                                       op0=ALU.mult, op1=ALU.add), inc=ve)
            b.srun("vector", lambda e: e.tensor_tensor(ycol, ycol, t1, op=ALU.mult), inc=ve)
            b.wait("vector", pe, xn_free[dbuf])
            b.srun("vector", lambda e: e.tensor_scalar(xn128[:, dbuf, :], xin_ap, mcol,
                                                       ycol, op0=ALU.subtract, op1=ALU.mult), inc=ve)
            return ve.n

        tpctr = [0]
        tpfree = [(0, 0), (0, 0)]
        xn_free = [0, 0]

        def emit_transpose(dbuf, dst_sel):
            for g in range(2):
                slot = tpctr[0] % 2
                tpctr[0] += 1
                pk = psA[2 + slot]
                b.wait("tensor", sc, tpfree[slot][0])
                b.wait("tensor", ve, tpfree[slot][1])
                for c4 in range(4):
                    c = 4 * g + c4
                    b.run("tensor", lambda e, c=c, c4=c4, pk=pk, dbuf=dbuf: e.transpose(
                        pk[:, 128 * c4:128 * (c4 + 1)],
                        xn128[:, dbuf, 128 * c:128 * (c + 1)], ident[:]),
                        inc=(pe if c4 == 3 else None))
                tmm = pe.n
                hi_ap, lo_ap = dst_sel(g)
                b.wait("scalar", pe, tmm)
                b.run("scalar", lambda e, pk=pk, hi_ap=hi_ap: e.copy(
                    hi_ap, pk.ap().rearrange("p (c n) -> p c n", c=4)), inc=sc)
                thi = sc.n
                b.wait("vector", sc, thi)
                b.run("vector", lambda e, pk=pk, hi_ap=hi_ap, lo_ap=lo_ap: e.scalar_tensor_tensor(
                    lo_ap, pk.ap().rearrange("p (c n) -> p c n", c=4), 1.0, hi_ap,
                    op0=ALU.mult, op1=ALU.subtract), inc=ve)
                tpfree[slot] = (sc.n, ve.n)
            xn_free[dbuf] = pe.n
            return sc.n

        r3h = lambda ap, c=8: ap.rearrange("p (c n) -> p c n", c=c)
        qstore = [0]
        xtstore = [0]
        qring2 = [0, 0]
        qtl_done = [0]
        ln_done = {}
        gd_ev = [0]
        gd_st = [0]
        wq_free = [0]
        sub = 0
        b.wait("vector", ve, t_gdw)
        b.srun("vector", lambda e: e.tensor_copy(gdwb[:, 0:8], gdw[:]), inc=ve)
        b.srun("vector", lambda e: e.scalar_tensor_tensor(
            gdwb[:, 8:16], gdw[:], 1.0, gdwb[:, 0:8],
            op0=ALU.mult, op1=ALU.subtract), inc=ve)
        t_gdwb = ve.n
        for t in range(4):
            for s4 in range(4):
                xb_ = sub % 2
                if sub >= 2:
                    b.wait("sync", ve, ln_done[sub - 2][0])
                    b.wait("sync", sc, ln_done[sub - 2][1])
                def _(e, t=t, s4=s4, xb_=xb_):
                    e.dma_start(out=xint[:, xb_, :],
                                in_=x_own[512 * t + 128 * s4:512 * t + 128 * (s4 + 1), :]
                                ).then_inc(ld.h, 16)
                E("sync", _)
                ld.n += 16
                b.wait("vector", ld, ld.n)
                b.wait("scalar", ld, ld.n)
                tln = emit_ln(xint[:, xb_, :], xb_)
                b.wait("tensor", ve, tln)
                b.wait("tensor", ve, t_ident)
                b.wait("scalar", st, xtstore[0])
                def dst_sel(g, s4=s4):
                    return (r3h(xnth)[:, 4 * g:4 * (g + 1), 128 * s4:128 * (s4 + 1)],
                            r3h(xntl)[:, 4 * g:4 * (g + 1), 128 * s4:128 * (s4 + 1)])
                tsc = emit_transpose(xb_, dst_sel)
                ln_done[sub] = (ve.n, sc.n)
                sub += 1
            t_xnTt = sc.n
            t_xnTtv = ve.n
            b.wait("sync", sc, t_xnTt)
            b.wait("sync", ve, t_xnTtv)
            def _(e, t=t):
                e.dma_start(out=xt_hbm.rearrange("l c p n -> p l c n")[:, 0, :, 512 * t:512 * (t + 1)],
                            in_=r3h(xnth)).then_inc(st.h, 16)
                e.dma_start(out=xt_hbm.rearrange("l c p n -> p l c n")[:, 1, :, 512 * t:512 * (t + 1)],
                            in_=r3h(xntl)).then_inc(st.h, 16)
            E("sync", _)
            st.n += 32
            xtstore[0] = st.n
            b.wait("tensor", sc, t_xnTt)
            b.wait("tensor", ve, max(gd_ev[0], t_gdwb, t_xnTtv))
            nmm = 0
            for c in range(8):
                for ph in range(3):
                    stat = gdwb[:, c:c + 1] if ph < 2 else gdwb[:, 8 + c:9 + c]
                    mov = (r3h(xnth)[:, c, :] if ph != 1 else r3h(xntl)[:, c, :])
                    nmm += 1
                    b.run("tensor", lambda e, stat=stat, mov=mov, nmm=nmm: e.matmul(
                        psA[0][0:1, 0:512], stat, mov,
                        start=(nmm == 1), stop=(nmm == 24)), inc=(pe if nmm == 24 else None))
            t_gd = pe.n
            b.wait("vector", pe, t_gd)
            b.wait("vector", st, gd_st[0])
            b.run("vector", lambda e: e.tensor_copy(gdtmp, psA[0][0:1, 0:512]), inc=ve)
            gd_ev[0] = ve.n
            b.wait("sync", ve, gd_ev[0])
            def _(e, t=t):
                e.dma_start(out=g_hbm[0:1, 512 * t:512 * (t + 1)], in_=gdtmp).then_inc(st.h, 16)
            E("sync", _)
            st.n += 16
            gd_st[0] = st.n
            for quarter in range(4):
                b.wait("sync", pe, wq_free[0])
                def _(e, quarter=quarter):
                    e.dma_start(out=r3h(wbig)[:, :, 0:256],
                                in_=wqkvT.rearrange("(c p) n -> p c n", p=128)[:, :, 256 * quarter:256 * (quarter + 1)]
                                ).then_inc(ld.h, 16)
                E("sync", _)
                ld.n += 16
                ld_wq = ld.n
                b.wait("vector", ld, ld_wq)
                b.wait("vector", pe, wq_free[0])
                b.srun("vector", lambda e: e.tensor_copy(r3h(wbh)[:, :, :], r3h(wbig)[:, :, 0:256]), inc=ve)
                b.srun("vector", lambda e: e.scalar_tensor_tensor(
                    r3h(wbl)[:, :, :], r3h(wbig)[:, :, 0:256], 1.0, r3h(wbh)[:, :, :],
                    op0=ALU.mult, op1=ALU.subtract), inc=ve)
                t_wq = ve.n
                b.wait("tensor", ve, t_wq)
                for q2 in range(2):
                    qc = 2 * quarter + q2
                    qs = qc % 2
                    b.wait("tensor", sc, qring2[qs])
                    nmm = 0
                    for c in range(8):
                        for ph in range(3):
                            stat = (r3h(wbh)[:, c, 128 * q2:128 * (q2 + 1)] if ph < 2
                                    else r3h(wbl)[:, c, 128 * q2:128 * (q2 + 1)])
                            mov = (r3h(xnth)[:, c, :] if ph != 1 else r3h(xntl)[:, c, :])
                            nmm += 1
                            b.run("tensor", lambda e, stat=stat, mov=mov, nmm=nmm, qs=qs: e.matmul(
                                psB[qs][:, 0:512], stat, mov,
                                start=(nmm == 1), stop=(nmm == 24)), inc=(pe if nmm == 24 else None))
                    tq = pe.n
                    b.wait("scalar", pe, tq)
                    b.wait("scalar", st, qstore[0])
                    b.wait("scalar", ve, qtl_done[0])
                    b.run("scalar", lambda e, qc=qc, qs=qs: e.activation(
                        qtile, psB[qs][:, 0:512], AF.Identity,
                        bias=c_bqk[:, qc:qc + 1], scale=1.0), inc=sc)
                    qring2[qs] = sc.n
                    b.srun("scalar", lambda e: e.activation(
                        qth, qtile, AF.Identity, bias=0.0, scale=1.0), inc=sc)
                    t_qth = sc.n
                    b.wait("vector", sc, t_qth)
                    b.run("vector", lambda e: e.scalar_tensor_tensor(
                        qtl, qtile, 1.0, qth, op0=ALU.mult, op1=ALU.subtract), inc=ve)
                    qtl_done[0] = ve.n
                    b.wait("sync", sc, t_qth)
                    b.wait("sync", ve, qtl_done[0])
                    def _(e, qc=qc, t=t):
                        e.dma_start(out=q_hbm[0, qc, :, 512 * t:512 * (t + 1)], in_=qth
                                    ).then_inc(st.h, 16)
                        e.dma_start(out=q_hbm[1, qc, :, 512 * t:512 * (t + 1)], in_=qtl
                                    ).then_inc(st.h, 16)
                    E("sync", _)
                    st.n += 32
                    qstore[0] = st.n
                wq_free[0] = pe.n
        t_gd_all = gd_ev[0]

        def finish():
            b.wait("sync", st, st.n)
            with nc.Block() as block:
                @block.sync
                def _(e):
                    for f in b.ops["sync"]:
                        f(e)

                @block.tensor
                def _(e):
                    for f in b.ops["tensor"]:
                        f(e)

                @block.vector
                def _(e):
                    for f in b.ops["vector"]:
                        f(e)

                @block.scalar
                def _(e):
                    for f in b.ops["scalar"]:
                        f(e)

                @block.gpsimd
                def _(e):
                    for f in b.ops["gpsimd"]:
                        f(e)
            return nc

        if stop == "A0":
            return finish()

        wvstg1 = A(big1, 0, 4096)
        b.wait("sync", pe, wq_free[0])
        def _(e):
            e.dma_start(out=r3h(wvstg1)[:, :, 0:512],
                        in_=wqkvT.rearrange("(c p) n -> p c n", p=128)[:, :, 2 * DM:2 * DM + 512]
                        ).then_inc(ld.h, 16)
        E("sync", _)
        ld.n += 16
        ld_wv0 = ld.n
        b.wait("vector", ld, ld_wv0)
        b.wait("vector", pe, wq_free[0])
        b.wait("vector", st, xtstore[0])
        b.wait("vector", sc, sc.n)
        b.srun("vector", lambda e: e.tensor_copy(r3h(wvh)[:, :, :], r3h(wvstg1)[:, :, 0:512]), inc=ve)
        b.srun("vector", lambda e: e.scalar_tensor_tensor(
            r3h(wvl)[:, :, :], r3h(wvstg1)[:, :, 0:512], 1.0, r3h(wvh)[:, :, :],
            op0=ALU.mult, op1=ALU.subtract), inc=ve)
        t_wv0 = ve.n
        vring = [0, 0]
        vst = [0]
        t_qevac_all = sc.n

        def emit_vmms(tc, half):
            hs = tc % 2
            b.wait("tensor", ve, max(vring[hs], t_wv0 if half == 0 else 0))
            if half == 0:
                b.wait("tensor", sc, t_qevac_all)
                b.wait("tensor", sc, ln_done[16 + tc][1])
                b.wait("tensor", ve, ln_done[16 + tc][0])
            nmm = 0
            for c in range(8):
                for ph in range(3):
                    stat = (r3h(xnTh)[:, c, 128 * tc:128 * (tc + 1)] if ph != 1
                            else r3h(xnTl)[:, c, 128 * tc:128 * (tc + 1)])
                    mov = (r3h(wvh)[:, c, :] if ph < 2 else r3h(wvl)[:, c, :])
                    nmm += 1
                    b.run("tensor", lambda e, stat=stat, mov=mov, nmm=nmm, hs=hs: e.matmul(
                        psB[hs][:, 0:512], stat, mov,
                        start=(nmm == 1), stop=(nmm == 24)), inc=(pe if nmm == 24 else None))
            return pe.n

        def emit_vevac(tc, half, tv_):
            hs = tc % 2
            b.wait("vector", pe, tv_)
            b.wait("vector", st, vst[0])
            b.run("vector", lambda e, half=half, hs=hs: e.scalar_tensor_tensor(
                vtile[:, 512 * half:512 * (half + 1)], psB[hs][:, 0:512], 1.0,
                c_bvb[:, 512 * half:512 * (half + 1)],
                op0=ALU.mult, op1=ALU.add), inc=ve)
            vring[hs] = ve.n
            b.wait("sync", ve, ve.n)
            def _(e, tc=tc, half=half):
                e.dma_start(out=v_hbm[128 * tc:128 * (tc + 1), 512 * half:512 * (half + 1)],
                            in_=vtile[:, 512 * half:512 * (half + 1)]).then_inc(st.h, 16)
            E("sync", _)
            st.n += 16
            vst[0] = st.n

        vpend = []
        for t in range(8):
            for s4 in range(4):
                xb_ = sub % 2
                b.wait("sync", ve, ln_done[sub - 2][0])
                b.wait("sync", sc, ln_done[sub - 2][1])
                def _(e, t=t, s4=s4, xb_=xb_):
                    e.dma_start(out=xint[:, xb_, :],
                                in_=x_seq[512 * t + 128 * s4:512 * t + 128 * (s4 + 1), :]
                                ).then_inc(ld.h, 16)
                E("sync", _)
                ld.n += 16
                b.wait("vector", ld, ld.n)
                b.wait("scalar", ld, ld.n)
                tln = emit_ln(xint[:, xb_, :], xb_)
                b.wait("tensor", ve, tln)
                col = 512 * t + 128 * s4
                def dst_sel(g, col=col):
                    return (r3h(xnTh)[:, 4 * g:4 * (g + 1), col:col + 128],
                            r3h(xnTl)[:, 4 * g:4 * (g + 1), col:col + 128])
                emit_transpose(xb_, dst_sel)
                ln_done[sub] = (ve.n, sc.n)
                sub += 1
                if vpend:
                    emit_vevac(*vpend.pop(0))
                tc = sub - 1 - 16
                vpend.append((tc, 0, emit_vmms(tc, 0)))
        while vpend:
            emit_vevac(*vpend.pop(0))
        t_xnT_done = sc.n
        t_xnT_donev = ve.n

        b.wait("sync", ve, t_wv0)
        def _(e):
            e.dma_start(out=r3h(wvstg1)[:, :, 0:512],
                        in_=wqkvT.rearrange("(c p) n -> p c n", p=128)[:, :, 2 * DM + 512:2 * DM + 1024]
                        ).then_inc(ld.h, 16)
        E("sync", _)
        ld.n += 16
        ld_wv = ld.n
        b.wait("vector", ld, ld_wv)
        b.wait("vector", pe, pe.n)
        b.srun("vector", lambda e: e.tensor_copy(r3h(wvh)[:, :, :], r3h(wvstg1)[:, :, 0:512]), inc=ve)
        b.srun("vector", lambda e: e.scalar_tensor_tensor(
            r3h(wvl)[:, :, :], r3h(wvstg1)[:, :, 0:512], 1.0, r3h(wvh)[:, :, :],
            op0=ALU.mult, op1=ALU.subtract), inc=ve)
        t_wv1 = ve.n
        b.wait("tensor", ve, t_wv1)
        for tc in range(32):
            tv_ = emit_vmms(tc, 1)
            emit_vevac(tc, 1, tv_)
        t_v_done = st.n

        b.wait("gpsimd", sc, t_xnT_done)
        b.wait("gpsimd", ve, ve.n)
        b.run("gpsimd", lambda e: e.memset(
            vh.rearrange("p (k c) -> p k c", c=65)[:, :, 64:65], 1.0), inc=gp)
        t_vones = gp.n
        b.wait("scalar", ve, ve.n)

        sringS4 = [[0, 0], [0, 0], [0, 0], [0, 0]]
        emrow_rd = [0, 0, 0, 0]
        emc_rd = [0] * NS
        uguard_pe = [0, 0]
        t_aev = [0, 0, 0, 0]
        asb_free = [0]

        for hp in range(8):
            b.wait("sync", pe, pe.n)
            def _(e, hp=hp):
                e.dma_start(out=r3(wkP)[:, :, :],
                            in_=wqkvT.rearrange("(c p) n -> p c n", p=128)[:, :, DM + 128 * hp:DM + 128 * (hp + 1)]
                            ).then_inc(ld.h, 16)
            E("sync", _)
            ld.n += 16
            ld_wk = ld.n
            b.wait("vector", ld, ld_wk)
            b.wait("vector", pe, pe.n)
            b.srun("vector", lambda e: e.tensor_copy(r3h(wkh)[:, :, :], r3(wkP)[:, :, :]), inc=ve)
            b.srun("vector", lambda e: e.scalar_tensor_tensor(
                r3h(wkl)[:, :, :], r3(wkP)[:, :, :], 1.0, r3h(wkh)[:, :, :],
                op0=ALU.mult, op1=ALU.subtract), inc=ve)
            t_wk = ve.n
            b.wait("tensor", ve, t_wk)
            b.wait("tensor", sc, sc.n)
            kring = [0, 0]
            stg_free = [0]
            for tt in range(8):
                ks = tt % 2
                b.wait("tensor", sc, kring[ks])
                nmm = 0
                for c in range(8):
                    for ph in range(3):
                        stat = (r3h(wkh)[:, c, :] if ph < 2 else r3h(wkl)[:, c, :])
                        mov = (r3h(xnTh)[:, c, 512 * tt:512 * (tt + 1)] if ph != 1
                               else r3h(xnTl)[:, c, 512 * tt:512 * (tt + 1)])
                        nmm += 1
                        b.run("tensor", lambda e, stat=stat, mov=mov, nmm=nmm, ks=ks: e.matmul(
                            psB[ks][:, 0:512], stat, mov,
                            start=(nmm == 1), stop=(nmm == 24)), inc=(pe if nmm == 24 else None))
                tk = pe.n
                b.wait("scalar", pe, tk)
                b.wait("scalar", ve, stg_free[0])
                if tt == 0:
                    b.wait("scalar", ve, t_wk)
                b.run("scalar", lambda e, hp=hp, ks=ks: e.activation(
                    f32stg, psB[ks][:, 0:512], AF.Identity,
                    bias=c_bqk[:, 8 + hp:9 + hp], scale=1.0), inc=sc)
                kring[ks] = sc.n
                kb0c = kbs[0][:, 512 * tt:512 * (tt + 1)]
                kb1c = kbs[1][:, 512 * tt:512 * (tt + 1)]
                b.srun("scalar", lambda e, kb0c=kb0c: e.activation(
                    kb0c[0:64, :], f32stg[0:64, :], AF.Identity, bias=0.0, scale=1.0), inc=sc)
                t_s2 = sc.n
                b.srun("scalar", lambda e: e.activation(
                    tmpb[64:128, :], f32stg[64:128, :], AF.Identity, bias=0.0, scale=1.0), inc=sc)
                t_s3 = sc.n
                b.srun("scalar", lambda e, kb1c=kb1c: e.activation(
                    kb1c[0:64, :], tmpb[64:128, :], AF.Identity, bias=0.0, scale=1.0), inc=sc)
                t_s4 = sc.n
                b.wait("vector", sc, t_s3)
                b.run("vector", lambda e, kb1c=kb1c: e.scalar_tensor_tensor(
                    kb1c[64:128, :], f32stg[64:128, :], 1.0, tmpb[64:128, :],
                    op0=ALU.mult, op1=ALU.subtract), inc=ve)
                b.run("vector", lambda e, kb0c=kb0c: e.scalar_tensor_tensor(
                    lo0t[0:64, :], f32stg[0:64, :], 1.0, kb0c[0:64, :],
                    op0=ALU.mult, op1=ALU.subtract), inc=ve)
                b.srun("vector", lambda e, kb0c=kb0c: e.tensor_copy(
                    kb0c[64:128, :], lo0t[0:64, :]), inc=ve)
                stg_free[0] = ve.n
            t_kprod = sc.n
            t_kb_ve = ve.n

            for a in range(2):
                h = 2 * hp + a
                base = 64 * a
                b.wait("sync", gp, t_vones)
                b.wait("sync", st, t_v_done)
                b.wait("sync", pe, max(uguard_pe))
                def _(e, h=h):
                    e.dma_start(out=vh.rearrange("p (k c) -> p k c", c=65)[:, :, 0:64],
                                in_=v_hbm.rearrange("(k p) n -> p k n", p=128)[:, :, 64 * h:64 * (h + 1)]
                                ).then_inc(ld.h, 16)
                E("sync", _)
                ld.n += 16
                ld_v = ld.n
                b.wait("sync", st, qstore[0])
                def _(e, hp=hp, a=a):
                    for l in range(2):
                        dst = qhi2 if l == 0 else qlo2
                        srcq = q_hbm[l, hp, 64 * a:64 * (a + 1), :]
                        e.dma_start(out=dst[0:64, :], in_=srcq).then_inc(ld.h, 16)
                        e.dma_start(out=dst[64:128, :], in_=srcq).then_inc(ld.h, 16)
                E("sync", _)
                ld.n += 64
                ld_q = ld.n

                emc = lnst[:, 6:7]
                t_emtr = 0
                sringQsc = [0, 0]
                sqc = 0
                for i in range(NS):
                    EK = 128 * (NKC - 2 * i)
                    nkt = (EK + 511) // 512
                    for kt in range(nkt):
                        width = min(512, EK - 512 * kt)
                        sq = sqc % 2
                        sqc += 1
                        psq = psB[1].ap()[:, 512 * sq:512 * sq + width]
                        b.wait("tensor", ve, sringS4[2 + sq][1])
                        b.wait("tensor", sc, sringS4[2 + sq][0])
                        if i == 0 and kt == 0:
                            b.wait("tensor", sc, t_kprod)
                            b.wait("tensor", ve, t_kb_ve)
                            b.wait("tensor", ld, ld_q)
                        kbm = kbs[a][:, 512 * kt:512 * kt + width]
                        b.run("tensor", lambda e, i=i, psq=psq, kbm=kbm: e.matmul(
                            psq, qhi2[:, 128 * i:128 * (i + 1)], kbm,
                            start=True, stop=False))
                        b.run("tensor", lambda e, i=i, psq=psq, kbm=kbm: e.matmul(
                            psq, qlo2[:, 128 * i:128 * (i + 1)], kbm,
                            start=False, stop=True), inc=pe)
                        t_mm = pe.n
                        b.wait("vector", pe, t_mm)
                        if kt == nkt - 1:
                            b.run("vector", lambda e, psq=psq, width=width: e.tensor_tensor(
                                psq[:, width - 256:width - 128],
                                psq[:, width - 256:width - 128],
                                masks[:, 256:384], op=ALU.add), inc=ve)
                            b.run("vector", lambda e, psq=psq, width=width: e.tensor_tensor(
                                psq[:, width - 128:width],
                                psq[:, width - 128:width],
                                masks[:, 384:512], op=ALU.add), inc=ve)
                        if nkt == 1:
                            b.wait("vector", sc, emc_rd[i])
                            b.srun("vector", lambda e, i=i, psq=psq: e.reduce_max(
                                mrun[:, 4 * i:4 * i + 1], psq, axis=mybir.AxisListType.X), inc=ve)
                        else:
                            b.srun("vector", lambda e, kt=kt, psq=psq: e.reduce_max(
                                mrun[:, 64 + 4 * kt:65 + 4 * kt], psq,
                                axis=mybir.AxisListType.X), inc=ve)
                        sringS4[2 + sq][1] = ve.n
                    if nkt > 1:
                        b.wait("vector", sc, emc_rd[i])
                        b.srun("vector", lambda e, i=i, nkt=nkt: e.reduce_max(
                            mrun[:, 4 * i:4 * i + 1],
                            mrun.ap()[:, 64:64 + 4 * nkt].rearrange(
                                "p (a b) -> p a b", b=4)[:, :, 0:1].rearrange(
                                "p a b -> p (a b)"),
                            axis=mybir.AxisListType.X), inc=ve)
                    t_mx = ve.n
                    b.wait("scalar", ve, t_mx)
                    b.wait("scalar", pe, t_emtr)
                    b.run("scalar", lambda e, i=i: e.activation(
                        emc, mrun[:, 4 * i:4 * i + 1], AF.Exp, scale=0.125), inc=sc)
                    t_emc = sc.n
                    emc_rd[i] = sc.n
                    sq = sqc % 2
                    sqc += 1
                    b.wait("tensor", ve, sringS4[2 + sq][1])
                    b.wait("tensor", sc, max(sringS4[2 + sq][0], t_emc))
                    b.run("tensor", lambda e, sq=sq: e.transpose(
                        psB[1].ap()[0:1, 512 * sq:512 * sq + 128], emc, ident[:]), inc=pe)
                    t_emtr = pe.n
                    b.wait("scalar", pe, t_emtr)
                    b.wait("scalar", ve, emrow_rd[i // 4])
                    b.run("scalar", lambda e, i=i, sq=sq: e.copy(
                        emrow[0:1, 128 * i:128 * (i + 1)],
                        psB[1].ap()[0:1, 512 * sq:512 * sq + 128]), inc=sc)
                    sringS4[2 + sq][0] = sc.n
                t_em = sc.n

                svstart = [True] * NW
                t_svw = [0] * NW
                t_expk = {}
                pS_map = {}
                sslot = [0]

                def emit_scores(k, a=a, h=h):
                    P = Pfn(k)
                    pref = 128 * P
                    nwin = (P + 3) // 4
                    ub = k % 2
                    for w in range(nwin):
                        width = min(512, pref - 512 * w)
                        slot = sslot[0] % 4
                        sslot[0] += 1
                        pS = psB[slot // 2].ap()[:, 512 * (slot % 2):512 * (slot % 2) + width]
                        pS_map[(k, w)] = pS
                        b.wait("tensor", sc, sringS4[slot][0])
                        b.wait("tensor", ve, sringS4[slot][1])
                        kbm = kbs[a][:, 128 * k:128 * (k + 1)]
                        b.run("tensor", lambda e, pS=pS, w=w, width=width, kbm=kbm: e.matmul(
                            pS, kbm, qhi2[:, 512 * w:512 * w + width],
                            start=True, stop=False))
                        b.run("tensor", lambda e, pS=pS, w=w, width=width, kbm=kbm: e.matmul(
                            pS, kbm, qlo2[:, 512 * w:512 * w + width],
                            start=False, stop=True), inc=pe)
                        t_mm = pe.n
                        lastw = (P - 1) // 4
                        if w == lastw:
                            off = (pref - 128) - 512 * w
                            mcol = 128 * (k % 2)
                            b.wait("vector", pe, t_mm)
                            b.run("vector", lambda e, off=off, mcol=mcol, pS=pS: e.tensor_tensor(
                                pS[:, off:off + 128], pS[:, off:off + 128],
                                masks[:, mcol:mcol + 128], op=ALU.add), inc=ve)
                            b.wait("scalar", ve, ve.n)
                        else:
                            b.wait("scalar", pe, t_mm)
                        b.wait("scalar", pe, uguard_pe[ub])
                        b.run("scalar", lambda e, w=w, width=width, pS=pS, ub=ub: e.activation(
                            ubuf[ub][:, 512 * w:512 * w + width], pS, AF.Exp, scale=0.125), inc=sc)
                        sringS4[slot][0] = sc.n
                        t_expk[(k, w)] = sc.n

                def emit_sv(k, a=a, h=h):
                    P = Pfn(k)
                    pref = 128 * P
                    nwin = (P + 3) // 4
                    ub = k % 2
                    for w in range(nwin):
                        width = min(512, pref - 512 * w)
                        b.wait("tensor", sc, t_expk[(k, w)])
                        if k == 0 and w == 0:
                            b.wait("tensor", ld, ld_v)
                        if svstart[w]:
                            b.wait("tensor", ve, t_aev[w])
                        b.run("tensor", lambda e, k=k, w=w, width=width, s=svstart[w], ub=ub: e.matmul(
                            psA[w][0:65, 0:width], vh[:, 65 * k:65 * (k + 1)],
                            ubuf[ub][:, 512 * w:512 * w + width],
                            start=s, stop=(k == KEND[w])), inc=pe)
                        svstart[w] = False
                        t_svw[w] = pe.n
                    uguard_pe[ub] = pe.n

                def emit_deaths(k, a=a, h=h):
                    P = Pfn(k)
                    nwin = (P + 3) // 4
                    for w in range(nwin):
                        if k == KEND[w]:
                            b.wait("vector", pe, t_svw[w])
                            b.wait("vector", st, st.n)
                            b.wait("vector", ve, asb_free[0])
                            b.run("vector", lambda e, w=w: e.tensor_copy(
                                asb65, psA[w][0:65, 0:512]), inc=ve)
                            t_aev[w] = ve.n
                            b.wait("tensor", ve, t_aev[w])
                            b.run("tensor", lambda e, w=w: e.matmul(
                                psA[w].ap()[0:1, 0:512],
                                unitv[0:65, 0:1], asb65[0:65, :],
                                start=True, stop=True), inc=pe)
                            t_ur = pe.n
                            b.wait("vector", pe, t_ur)
                            b.wait("vector", sc, t_em)
                            b.srun("vector", lambda e, h=h, w=w: e.scalar_tensor_tensor(
                                denb, emrow[0:1, 512 * w:512 * (w + 1)],
                                c_gh[0:1, h:h + 1],
                                psA[w].ap()[0:1, 0:512],
                                op0=ALU.mult, op1=ALU.add), inc=ve)
                            emrow_rd[w] = ve.n
                            b.srun("vector", lambda e: e.reciprocal(denb, denb), inc=ve)
                            t_den = ve.n
                            b.wait("tensor", ve, t_den)
                            b.run("tensor", lambda e, w=w: e.matmul(
                                psA[w].ap()[0:64, 0:512],
                                ones1[0:1, 0:64], denb,
                                start=True, stop=True), inc=pe)
                            t_rb = pe.n
                            b.wait("vector", pe, t_rb)
                            b.srun("vector", lambda e, w=w: e.tensor_tensor(
                                asb65[0:64, :], asb65[0:64, :],
                                psA[w].ap()[0:64, 0:512], op=ALU.mult), inc=ve)
                            asb_free[0] = ve.n
                            t_aev[w] = ve.n
                            b.wait("sync", ve, asb_free[0])
                            def _(e, w=w, h=h):
                                e.dma_start(out=a_hbm[h // 2, 64 * (h % 2):64 * (h % 2) + 64,
                                                      512 * w:512 * (w + 1)],
                                            in_=asb65[0:64, :]).then_inc(st.h, 16)
                            E("sync", _)
                            st.n += 16

                emit_scores(0)
                for k in range(NKC):
                    if k + 1 < NKC:
                        emit_scores(k + 1)
                    emit_sv(k)
                    emit_deaths(k)
            if stop == "H1" and hp == 0:
                return finish()
        t_attn_done = st.n

        def _(e):
            e.dma_start(out=c_cA[:], in_=sb2[:]).then_inc(ld.h, 16)
            e.dma_start(out=c_b2t[:, 0:8], in_=b2e[0]).then_inc(ld.h, 16)
            e.dma_start(out=c_b2t[:, 8:16], in_=b2e[1]).then_inc(ld.h, 16)
            e.dma_start(out=gdif, in_=g_hbm[:]).then_inc(ld.h, 16)
        E("sync", _)
        ld.n += 64
        ld_b2 = ld.n
        b.wait("sync", st, t_attn_done)
        b.wait("gpsimd", st, t_attn_done)
        b.wait("vector", ld, ld_b2)
        b.srun("vector", lambda e: e.tensor_tensor(c_cA[:], c_cA[:], c_b2t[:, 0:8], op=ALU.add), inc=ve)
        b.srun("vector", lambda e: e.tensor_tensor(c_cA[:], c_cA[:], c_b2t[:, 8:16], op=ALU.add), inc=ve)
        b.srun("vector", lambda e: e.tensor_scalar(c_rb1[:], c_b1e[:], 0.0, 0.0,
                                                   op0=ALU.max, op1=ALU.add), inc=ve)
        mstg = big3.ap()[0:1, 0:2048]
        b.run("vector", lambda e: e.tensor_scalar(
            mstg, gdif, 0.0, 1.0, op0=ALU.is_gt, op1=ALU.mult), inc=ve)
        t_gdif_read = ve.n
        b.wait("tensor", ve, t_gdif_read)
        for cc in range(4):
            b.run("tensor", lambda e, cc=cc: e.matmul(
                psB[cc % 2][:, 0:512], ones1[:],
                mstg[0:1, 512 * cc:512 * (cc + 1)],
                start=True, stop=True), inc=pe)
            t_b = pe.n
            b.wait("vector", pe, t_b)
            b.run("vector", lambda e, cc=cc: e.tensor_copy(
                m1b[:, 512 * cc:512 * (cc + 1)], psB[cc % 2][:, 0:512]), inc=ve)
        t_mask_b = ve.n

        paths = [(w1s, w2s, c_b1s.ap(), None),
                 (w1e0, w2e0, c_b1e.ap()[:, 0:32], (c_rb1.ap()[:, 0:32], True)),
                 (w1e1, w2e1, c_b1e.ap()[:, 32:64], (c_rb1.ap()[:, 32:64], False))]
        CH = [(ip, hc) for ip in range(3) for hc in range(8)]

        wfree = [0, 0]
        w2free = [0, 0]
        H_last = [t_gdif_read, 0]
        xfree = [0, 0]
        ofring = [0, 0]
        hring = [t_mask_b, t_mask_b]
        first_acc = True
        sc_w1 = {}; ve_w1 = {}; sc_w2 = {}; ve_w2 = {}
        pend = {}

        def piece_views(cidx, j):
            wb = cidx % 2
            ip, hc = CH[cidx]
            w1x, w2x = paths[ip][0], paths[ip][1]
            if j < 2:
                hi = w1c[wb][0][:, 2048 * j:2048 * (j + 1)]
                lo = w1c[wb][1][:, 2048 * j:2048 * (j + 1)]
                src = w1x.rearrange("(c p) n -> p c n", p=128)[
                    :, 4 * j:4 * (j + 1), 512 * hc:512 * (hc + 1)]
                a = 4
            else:
                s_ = j - 2
                hi = w2c[wb][0][:, 2048 * s_:2048 * (s_ + 1)]
                lo = w2c[wb][1][:, 2048 * s_:2048 * (s_ + 1)]
                src = w2x.rearrange("(hc s p) n -> hc p s n", p=128, s=4)[hc][
                    :, 2 * s_:2 * (s_ + 1), :]
                a = 2
            return hi, lo, src, a

        def emit_dma_cast(cidx, j):
            wb = cidx % 2
            hi, lo, src, a = piece_views(cidx, j)
            h_ = j % 2
            b.wait("sync", ve, H_last[h_])
            def _(e, src=src, h_=h_, a=a):
                e.dma_start(out=Hbuf[h_].rearrange("p (a n) -> p a n", a=a),
                            in_=src).then_inc(ld.h, 16)
            E("sync", _)
            ld.n += 16
            b.wait("scalar", ld, ld.n)
            b.wait("scalar", pe, (wfree[wb] if j < 2 else w2free[wb]))
            b.run("scalar", lambda e, hi=hi, h_=h_: e.activation(
                hi, Hbuf[h_], AF.Identity, bias=0.0, scale=1.0), inc=sc)
            pend[(cidx, j)] = sc.n

        def emit_stt(cidx, j):
            wb = cidx % 2
            hi, lo, src, a = piece_views(cidx, j)
            h_ = j % 2
            b.wait("vector", sc, pend[(cidx, j)])
            b.wait("vector", pe, (wfree[wb] if j < 2 else w2free[wb]))
            b.run("vector", lambda e, hi=hi, lo=lo, h_=h_: e.scalar_tensor_tensor(
                lo, Hbuf[h_], 1.0, hi, op0=ALU.mult, op1=ALU.subtract), inc=ve)
            H_last[h_] = ve.n
            if j == 1:
                sc_w1[cidx] = pend[(cidx, 1)]; ve_w1[cidx] = ve.n
            if j == 3:
                sc_w2[cidx] = pend[(cidx, 3)]; ve_w2[cidx] = ve.n

        for j in range(4):
            emit_dma_cast(0, j)
            emit_stt(0, j)

        for cidx, (ip, hc) in enumerate(CH):
            wb_ = cidx % 2
            w1x, w2x, b1ap, maskinfo = paths[ip]
            nxt = cidx + 1 < len(CH)
            if nxt:
                emit_dma_cast(cidx + 1, 0)
                emit_dma_cast(cidx + 1, 1)
            for t in range(4):
                xb_ = (4 * cidx + t) % 2
                b.wait("gpsimd", pe, xfree[xb_])
                def _(e, t=t, xb_=xb_):
                    e.dma_start(out=r3h(xot[xb_][0])[:, :, :],
                                in_=xt_hbm.rearrange("l c p n -> p l c n")[:, 0, :, 512 * t:512 * (t + 1)]
                                ).then_inc(lx.h, 16)
                    e.dma_start(out=r3h(xot[xb_][1])[:, :, :],
                                in_=xt_hbm.rearrange("l c p n -> p l c n")[:, 1, :, 512 * t:512 * (t + 1)]
                                ).then_inc(lx.h, 16)
                E("gpsimd", _)
                lx.n += 32
                ld_x = lx.n
                if t == 0:
                    b.wait("tensor", ve, ve_w1[cidx])
                    b.wait("tensor", sc, sc_w1[cidx])
                b.wait("tensor", lx, ld_x)
                for su in range(4):
                    hs = su % 2
                    b.wait("tensor", ve, hring[hs])
                    nmm = 0
                    for c in range(8):
                        for ph in range(3):
                            stat = (r3h(w1c[wb_][0])[:, c, 128 * su:128 * (su + 1)] if ph < 2
                                    else r3h(w1c[wb_][1])[:, c, 128 * su:128 * (su + 1)])
                            mov = (r3h(xot[xb_][0])[:, c, :] if ph != 1
                                   else r3h(xot[xb_][1])[:, c, :])
                            nmm += 1
                            b.run("tensor", lambda e, stat=stat, mov=mov, nmm=nmm, hs=hs: e.matmul(
                                psB[hs][:, 0:512], stat, mov,
                                start=(nmm == 1), stop=(nmm == 24)), inc=(pe if nmm == 24 else None))
                    th = pe.n
                    b.wait("vector", pe, th)
                    if maskinfo is None:
                        b.srun("vector", lambda e, hs=hs, b1ap=b1ap, hc=hc, su=su: e.tensor_scalar(
                            hself, psB[hs][:, 0:512],
                            b1ap[:, 4 * hc + su:4 * hc + su + 1], 0.0,
                            op0=ALU.add, op1=ALU.max), inc=ve)
                    else:
                        rb1ap, swap = maskinfo
                        b.wait("vector", ve, t_mask_b)
                        b.srun("vector", lambda e, hs=hs, b1ap=b1ap, hc=hc, su=su: e.tensor_scalar(
                            relscr, psB[hs][:, 0:512],
                            b1ap[:, 4 * hc + su:4 * hc + su + 1], 0.0,
                            op0=ALU.add, op1=ALU.max), inc=ve)
                        if swap:
                            b.srun("vector", lambda e, su=su, t=t, rb1ap=rb1ap, hc=hc: e.select(
                                hself, m1b[:, 512 * t:512 * (t + 1)],
                                rb1ap[:, 4 * hc + su:4 * hc + su + 1].to_broadcast((128, 512)),
                                relscr), inc=ve)
                        else:
                            b.srun("vector", lambda e, su=su, t=t, rb1ap=rb1ap, hc=hc: e.select(
                                hself, m1b[:, 512 * t:512 * (t + 1)], relscr,
                                rb1ap[:, 4 * hc + su:4 * hc + su + 1].to_broadcast((128, 512))),
                                inc=ve)
                    b.srun("vector", lambda e, su=su: e.tensor_copy(
                        r3h(hbh, 4)[:, su, :], hself), inc=ve)
                    b.srun("vector", lambda e, su=su: e.scalar_tensor_tensor(
                        r3h(hbl, 4)[:, su, :], hself, 1.0, r3h(hbh, 4)[:, su, :],
                        op0=ALU.mult, op1=ALU.subtract), inc=ve)
                    hring[hs] = ve.n
                t_h = ve.n
                b.wait("tensor", ve, t_h)
                if t == 0:
                    b.wait("tensor", ve, ve_w2[cidx])
                    b.wait("tensor", sc, sc_w2[cidx])
                for of in range(8):
                    os_ = of % 2
                    b.wait("tensor", ve, ofring[os_])
                    nmm = 0
                    for su in range(4):
                        for ph in range(3):
                            stat = (r3h(w2c[wb_][0], 4)[:, su, 128 * of:128 * (of + 1)] if ph < 2
                                    else r3h(w2c[wb_][1], 4)[:, su, 128 * of:128 * (of + 1)])
                            mov = (r3h(hbh, 4)[:, su, :] if ph != 1 else r3h(hbl, 4)[:, su, :])
                            nmm += 1
                            b.run("tensor", lambda e, stat=stat, mov=mov, nmm=nmm, os_=os_: e.matmul(
                                psA[os_][:, 0:512], stat, mov,
                                start=(nmm == 1), stop=(nmm == 12)), inc=(pe if nmm == 12 else None))
                    tw2 = pe.n
                    b.wait("vector", pe, tw2)
                    accs = r3(ACC)[:, of, 512 * t:512 * (t + 1)]
                    if first_acc:
                        b.run("vector", lambda e, accs=accs, os_=os_: e.tensor_copy(
                            accs, psA[os_][:, 0:512]), inc=ve)
                    else:
                        b.run("vector", lambda e, accs=accs, os_=os_: e.scalar_tensor_tensor(
                            accs, psA[os_][:, 0:512], 1.0, accs,
                            op0=ALU.mult, op1=ALU.add), inc=ve)
                    ofring[os_] = ve.n
                xfree[xb_] = pe.n
                if nxt:
                    if t == 0:
                        emit_stt(cidx + 1, 0)
                    elif t == 1:
                        emit_stt(cidx + 1, 1)
                        emit_dma_cast(cidx + 1, 2)
                    elif t == 2:
                        emit_stt(cidx + 1, 2)
                        emit_dma_cast(cidx + 1, 3)
                    else:
                        emit_stt(cidx + 1, 3)
            if cidx == 0:
                first_acc = False
            wfree[wb_] = pe.n
            w2free[wb_] = pe.n
        t_ffn = ve.n

        b.wait("sync", pe, max(wfree + w2free))
        def _(e):
            e.dma_start(out=woA.rearrange("p (c n) -> p c n", c=8),
                        in_=woT.rearrange("(c p) n -> p c n", p=128)).then_inc(ld.h, 16)
        E("sync", _)
        ld.n += 16
        ld_wo = ld.n
        sg_done = [0]

        woring = [0, 0]
        ost = [0]
        for t in range(4):
            ab_ = t % 2
            b.wait("sync", pe, xfree[ab_])
            def _(e, t=t, ab_=ab_):
                e.dma_start(out=r3(atile[ab_])[:, :, :],
                            in_=a_hbm.rearrange("c p n -> p c n")[:, :, 512 * t:512 * (t + 1)]
                            ).then_inc(ld.h, 16)
            E("sync", _)
            ld.n += 16
            ld_a = ld.n
            b.wait("tensor", ld, ld_a)
            b.wait("tensor", ld, ld_wo)
            if t == 0:
                b.wait("tensor", ve, t_ffn)
                b.wait("tensor", st, t_attn_done)
            for of in range(8):
                ws_ = of % 2
                b.wait("tensor", sc, woring[ws_])
                for c in range(8):
                    wsrc = woA.rearrange("p (cc n) -> p cc n", cc=8)[:, c, 128 * of:128 * (of + 1)]
                    b.run("tensor", lambda e, wsrc=wsrc, ws_=ws_, ab_=ab_, c=c: e.matmul(
                        psB[ws_][:, 0:512], wsrc, r3(atile[ab_])[:, c, :],
                        start=(c == 0), stop=(c == 7)), inc=(pe if c == 7 else None))
                ts_ = pe.n
                b.wait("scalar", pe, ts_)
                b.wait("scalar", ve, sg_done[0])
                b.run("scalar", lambda e, of=of, ws_=ws_: e.activation(
                    stile, psB[ws_][:, 0:512], AF.Identity, bias=c_bo[:, of:of + 1],
                    scale=1.0), inc=sc)
                woring[ws_] = sc.n
                b.wait("scalar", ve, t_ffn)
                b.run("scalar", lambda e, of=of, t=t: e.activation(
                    gtile, r3(ACC)[:, of, 512 * t:512 * (t + 1)], AF.Silu,
                    bias=c_cA[:, of:of + 1], scale=1.0), inc=sc)
                tg = sc.n
                b.wait("sync", ve, ve.n)
                def _(e, of=of, t=t):
                    e.dma_start(out=xt_in, in_=x_ownT[of, :, 512 * t:512 * (t + 1)]
                                ).then_inc(ld.h, 16)
                E("sync", _)
                ld.n += 16
                ld_xt = ld.n
                b.wait("vector", sc, tg)
                b.run("vector", lambda e: e.tensor_tensor(sgt, stile, gtile, op=ALU.mult), inc=ve)
                sg_done[0] = ve.n
                b.wait("vector", ld, ld_xt)
                b.wait("vector", st, ost[0])
                b.srun("vector", lambda e: e.tensor_tensor(obuf, sgt, xt_in, op=ALU.add), inc=ve)
                t_o = ve.n
                b.wait("sync", ve, t_o)
                def _(e, of=of, t=t):
                    e.dma_start(out=out[of, :, 512 * t:512 * (t + 1)], in_=obuf
                                ).then_inc(st.h, 16)
                E("sync", _)
                st.n += 16
                ost[0] = st.n
            xfree[ab_] = pe.n
        b.wait("sync", st, st.n)
        return finish()
    return nc


def _prep_core(inputs, bseq, p):
    x = np.ascontiguousarray(np.asarray(inputs["L0_attn_inputs"][bseq], dtype=np.float32))
    qidx = [31 - 2 * i - p for i in range(NS)]
    rows = np.concatenate([np.arange(128 * qi, 128 * qi + 128) for qi in qidx])
    x_own = np.ascontiguousarray(x[rows])
    x_ownT = np.ascontiguousarray(x_own.T.reshape(8, 128, TOK))
    mk = lambda a: np.ascontiguousarray(np.asarray(a, dtype=np.float32))
    dt = np.broadcast_to(np.array([[128.0 * (p - 1), 128.0 * p]], np.float32), (128, 2))
    return {
        "x_seq": x, "x_own": x_own, "x_ownT": x_ownT,
        "wqkvT": mk(inputs["L0_attn_Wqkv_w"].T),
        "bqk": mk(np.asarray(inputs["L0_attn_Wqkv_b"])[0:2048].reshape(16, 128).T),
        "bv": mk(np.broadcast_to(np.asarray(inputs["L0_attn_Wqkv_b"])[2048:3072].reshape(1, DM), (128, DM))),
        "woT": mk(inputs["L0_attn_Wo_w"].T),
        "bo": mk(np.asarray(inputs["L0_attn_Wo_b"]).reshape(8, 128).T),
        "ghost": mk(np.asarray(inputs["L0_attn_ghost"]).reshape(1, H)),
        "gateT": mk(np.asarray(inputs["L0_ffn_gate_w"]).T.reshape(8, 128, 2)),
        "w1s": mk(inputs["L0_ffn_sw1"]),
        "b1s": mk(np.asarray(inputs["L0_ffn_sb1"]).reshape(32, 128).T),
        "w2s": mk(inputs["L0_ffn_sw2"]),
        "sb2": mk(np.asarray(inputs["L0_ffn_sb2"]).reshape(8, 128).T),
        "w1e0": mk(inputs["L0_ffn_w1"][0]), "w1e1": mk(inputs["L0_ffn_w1"][1]),
        "b1e": mk(np.asarray(inputs["L0_ffn_b1"]).reshape(2, 32, 128).transpose(0, 2, 1)),
        "w2e0": mk(inputs["L0_ffn_w2"][0]), "w2e1": mk(inputs["L0_ffn_w2"][1]),
        "b2e": mk(np.asarray(inputs["L0_ffn_b2"]).reshape(2, 8, 128).transpose(0, 2, 1)),
        "dtab": dt,
    }


def kernel(**inputs):
    nc = build_nc()
    in_maps = []
    for core in range(8):
        bseq, p = core // 2, core % 2
        in_maps.append(_prep_core(inputs, bseq, p))
    res = run_bass_kernel_spmd(nc, in_maps, core_ids=list(range(8)))
    out = np.empty((4, S, DM), np.float32)
    for core in range(8):
        bseq, p = core // 2, core % 2
        shard = np.asarray(res.results[core]["out"])
        own = shard.reshape(DM, TOK).T
        for i in range(NS):
            qi = 31 - 2 * i - p
            out[bseq, 128 * qi:128 * qi + 128] = own[128 * i:128 * (i + 1)]
    return out
